# version 11
# baseline (speedup 1.0000x reference)
import contextlib
import numpy as np
import concourse.bass as bass
import concourse.mybir as mybir
from concourse.bass_utils import run_bass_kernel_spmd

F32 = mybir.dt.float32
BF16 = mybir.dt.bfloat16
U8 = mybir.dt.uint8
AF = mybir.ActivationFunctionType
OP = mybir.AluOpType
AX = mybir.AxisListType

NCORE = 8
T = 2048
HALO = 128
TTOT = T + HALO
H = 1024
DI = 2048
NST = 16
NEXP = 8
DFF = 3072
SFF = 1024
NPAN = TTOT // 128
OTW = 512
OTS = [(0, 128)] + [(HALO + OTW * i, OTW) for i in range(T // OTW)]
FTS = [(HALO + 512 * i, 512) for i in range(4)]

DEBUG = False


class Ctr:
    __slots__ = ("sem", "v")

    def __init__(self, sem):
        self.sem = sem
        self.v = 0


class Sched:
    def __init__(self):
        self.items = []

    def op(self, eng, fn):
        self.items.append((eng, fn))

    def wait(self, eng, ctr, val=None):
        v = ctr.v if val is None else val
        if v <= 0:
            return
        sem = ctr.sem
        self.items.append((eng, lambda e: e.wait_ge(sem, v)))

    def dma(self, eng, ctr, out, in_):
        ctr.v += 16
        sem = ctr.sem
        self.items.append((eng, lambda e: e.dma_start(out=out, in_=in_).then_inc(sem, 16)))
        return ctr.v

    def fence(self, eng, ctr):
        ctr.v += 1
        sem = ctr.sem
        self.items.append((eng, lambda e: e.drain().then_inc(sem, 1)))
        return ctr.v


ENGS = ("sync", "vector", "scalar", "tensor", "gpsimd")


def build_nc(debug=False):
    nc = bass.Bass("TRN2", target_bir_lowering=False)

    d_xin = nc.dram_tensor("xin", [TTOT, H], F32, kind="ExternalInput").ap()
    d_umask = nc.dram_tensor("umask", [128, 1], F32, kind="ExternalInput").ap()
    d_winT = nc.dram_tensor("winT", [H, 4096], F32, kind="ExternalInput").ap()
    d_inb = nc.dram_tensor("inb", [128, 32], F32, kind="ExternalInput").ap()
    d_convw = nc.dram_tensor("convw", [16, 128, 4], F32, kind="ExternalInput").ap()
    d_convb = nc.dram_tensor("convb", [128, 16], F32, kind="ExternalInput").ap()
    d_xpT = nc.dram_tensor("xpT", [DI, 96], F32, kind="ExternalInput").ap()
    d_dtpT = nc.dram_tensor("dtpT", [64, DI], F32, kind="ExternalInput").ap()
    d_dtpb = nc.dram_tensor("dtpb", [128, 16], F32, kind="ExternalInput").ap()
    d_alog = nc.dram_tensor("alog", [DI, NST], F32, kind="ExternalInput").ap()
    d_dvec = nc.dram_tensor("dvec", [128, 16], F32, kind="ExternalInput").ap()
    d_outT = nc.dram_tensor("outT", [DI, H], F32, kind="ExternalInput").ap()
    d_outb = nc.dram_tensor("outb", [128, 8], F32, kind="ExternalInput").ap()
    d_lnw = nc.dram_tensor("lnw", [128, 8], F32, kind="ExternalInput").ap()
    d_lnb = nc.dram_tensor("lnb", [128, 8], F32, kind="ExternalInput").ap()
    d_gam = nc.dram_tensor("gam", [128, 8], F32, kind="ExternalInput").ap()
    d_bet = nc.dram_tensor("bet", [128, 8], F32, kind="ExternalInput").ap()
    d_rT = nc.dram_tensor("rT", [H, 8], F32, kind="ExternalInput").ap()
    d_guT = nc.dram_tensor("guT", [NEXP, H, 2 * DFF], F32, kind="ExternalInput").ap()
    d_dnT = nc.dram_tensor("dnT", [NEXP, DFF, H], F32, kind="ExternalInput").ap()
    d_sguT = nc.dram_tensor("sguT", [H, 2 * SFF], F32, kind="ExternalInput").ap()
    d_sdnT = nc.dram_tensor("sdnT", [SFF, H], F32, kind="ExternalInput").ap()
    d_eyeb = nc.dram_tensor("eyeb", [128, 128], BF16, kind="ExternalInput").ap()
    d_eyef = nc.dram_tensor("eyef", [128, 128], F32, kind="ExternalInput").ap()
    d_ones = nc.dram_tensor("ones1", [1, 128], F32, kind="ExternalInput").ap()
    d_out = nc.dram_tensor("out", [T, H], F32, kind="ExternalOutput").ap()
    d_gsp = nc.dram_tensor("gspill", [T, H], BF16, kind="Internal").ap()
    b_guT = nc.dram_tensor("b_guT", [NEXP, H, 2 * DFF], BF16, kind="Internal").ap()
    b_dnT = nc.dram_tensor("b_dnT", [NEXP, DFF, H], BF16, kind="Internal").ap()
    b_sguT = nc.dram_tensor("b_sguT", [H, 2 * SFF], BF16, kind="Internal").ap()
    b_sdnT = nc.dram_tensor("b_sdnT", [SFF, H], BF16, kind="Internal").ap()
    b_winT = nc.dram_tensor("b_winT", [H, 4096], BF16, kind="Internal").ap()
    b_outT = nc.dram_tensor("b_outT", [DI, H], BF16, kind="Internal").ap()
    dbg = {}
    if debug:
        for nm, shp, dt in [("stats", [128, 8 * NPAN], F32), ("xn", [TTOT, H], F32), ("z", [H, TTOT], BF16),
                            ("g", [T, H], BF16), ("comb", [T, 8], F32),
                            ("xs", [DI, TTOT], BF16), ("dt", [DI, TTOT], BF16),
                            ("ym", [DI, TTOT], BF16), ("spre", [T, H], BF16)]:
            dbg[nm] = nc.dram_tensor("dbg_" + nm, shp, dt, kind="ExternalOutput").ap()

    S = Sched()

    with contextlib.ExitStack() as ctx:
        def mk_alloc(c):
            def sb(name, shape, dt):
                return c.enter_context(nc.sbuf_tensor(name, shape, dt))
            return sb

        sb = mk_alloc(ctx)

        def psum(name, shape, dt):
            return ctx.enter_context(nc.psum_tensor(name, shape, dt))

        def sem(name):
            return Ctr(ctx.enter_context(nc.semaphore(name)))

        zT = [sb(f"zT{c}", [128, TTOT], BF16) for c in range(8)]
        s_lnw = sb("s_lnw", [128, 8], F32)
        s_lnb = sb("s_lnb", [128, 8], F32)
        s_gam = sb("s_gam", [128, 8], F32)
        s_bet = sb("s_bet", [128, 8], F32)
        s_inb = sb("s_inb", [128, 32], F32)
        s_convb = sb("s_convb", [128, 16], F32)
        s_dtpb = sb("s_dtpb", [128, 16], F32)
        s_dvec = sb("s_dvec", [128, 16], F32)
        s_outb = sb("s_outb", [128, 8], F32)
        s_convw = [sb(f"s_convw{c}", [128, 4], F32) for c in range(16)]
        s_negA = sb("s_negA", [128, 16 * NST], F32)
        s_eyeb = sb("s_eyeb", [128, 128], BF16)
        s_ones = sb("s_ones", [1, 128], F32)
        s_umask = sb("s_umask", [128, 1], F32)
        s_xpT = [sb(f"s_xpT{c}", [128, 96], BF16) for c in range(16)]
        s_dtpT = sb("s_dtpT", [64, DI], F32)
        s_rT = [sb(f"s_rT{c}", [128, 8], BF16) for c in range(8)]
        carry = sb("carry", [128, 16 * NST], F32)
        convh = sb("convh", [128, 3 * 16], F32)
        mrs = sb("mrs", [128, 4 * NPAN], F32)
        scr2 = sb("scr2", [128, H], F32)

        psA = psum("psA", [128, 512], F32)
        psB = psum("psB", [128, 512], F32)
        psC = psum("psC", [128, 512], F32)
        psD = psum("psD", [128, 512], F32)
        psE = psum("psE", [128, 512], F32)
        psR = psum("psR", [128, 512], F32)
        psT = psum("psT", [128, 1024], BF16)
        psTf = psum("psTf", [128, 128], F32)

        c_ld = sem("c_ld")
        c_row = sem("c_row")
        c_st = sem("c_st")
        c_gs = sem("c_gs")
        c_v = sem("c_v")
        c_s = sem("c_s")
        c_t = sem("c_t")
        c_g = sem("c_g")
        c_dbg = sem("c_dbg")
        CTRS = {"sync": c_gs, "vector": c_v, "scalar": c_s, "tensor": c_t, "gpsimd": c_g}

        def barrier():
            marks = {e: S.fence(e, CTRS[e]) for e in ENGS}
            for e in ENGS:
                for e2 in ENGS:
                    if e2 != e:
                        S.wait(e, CTRS[e2], marks[e2])

        stg_mark = {}
        wbf_mark = {}

        for dst, src in [(s_lnw, d_lnw), (s_lnb, d_lnb), (s_gam, d_gam), (s_bet, d_bet),
                         (s_inb, d_inb), (s_convb, d_convb), (s_dtpb, d_dtpb),
                         (s_dvec, d_dvec), (s_outb, d_outb), (s_eyeb, d_eyeb),
                         (s_ones, d_ones), (s_umask, d_umask),
                         (s_dtpT, d_dtpT)]:
            S.dma("sync", c_ld, dst[:], src)
        for c in range(16):
            S.dma("sync", c_ld, s_convw[c][:], d_convw[c])
        for c in range(16):
            S.dma("sync", c_ld, s_negA[:, 16 * c:16 * c + 16], d_alog[128 * c:128 * (c + 1), :])
        ld_pre = c_ld.v

        with contextlib.ExitStack() as ctx0:
            sb0 = mk_alloc(ctx0)
            xpan = [sb0(f"xpan{i}", [128, H], F32) for i in range(2)]
            zpan = sb0("zpan", [128, H], BF16)
            stats = sb0("stats", [128, 4 * NPAN], F32)
            scr1 = sb0("scr1", [128, H], F32)
            tiny = sb0("tiny", [128, 8 * NPAN], F32)
            wcast = [sb0(f"wcast{i}", [128, 128], F32) for i in range(2)]

            for c in range(16):
                if c >= 2:
                    S.wait("sync", c_g, stg_mark.get(("wc", c % 2), 0))
                S.dma("sync", c_ld, wcast[c % 2][:, 0:96], d_xpT[128 * c:128 * (c + 1), :])
                S.wait("gpsimd", c_ld)
                dst, srcv = s_xpT[c], wcast[c % 2]
                S.op("gpsimd", lambda e, d=dst, s=srcv: e.tensor_copy(out=d[:], in_=s[:, 0:96]))
                stg_mark[("wc", c % 2)] = S.fence("gpsimd", c_g)
            for c in range(8):
                S.wait("sync", c_g, stg_mark[("wc", c % 2)])
                S.dma("sync", c_ld, wcast[c % 2][:, 0:8], d_rT[128 * c:128 * (c + 1), :])
                S.wait("gpsimd", c_ld)
                dst, srcv = s_rT[c], wcast[c % 2]
                S.op("gpsimd", lambda e, d=dst, s=srcv: e.tensor_copy(out=d[:], in_=s[:, 0:8]))
                stg_mark[("wc", c % 2)] = S.fence("gpsimd", c_g)

            cstg = [sb0(f"cstg{i}", [128, 2048], F32) for i in range(4)]
            cbf = [sb0(f"cbf{i}", [128, 2048], BF16) for i in range(4)]
            c_wc = sem("c_wc")
            cast_jobs = []
            for e_ in range(NEXP):
                cast_jobs.append((f"gu{e_}", d_guT[e_], b_guT[e_], H, 2 * DFF))
                cast_jobs.append((f"dn{e_}", d_dnT[e_], b_dnT[e_], DFF, H))
            cast_jobs.append(("sgu", d_sguT, b_sguT, H, 2 * SFF))
            cast_jobs.append(("sdn", d_sdnT, b_sdnT, SFF, H))
            cast_jobs.append(("win", d_winT, b_winT, H, 4096))
            cast_jobs.append(("out", d_outT, b_outT, DI, H))
            cast_done = {}
            wi = 0
            wc_mark = {}
            wst_mark = {}
            for (jname, srcT, dstT, RR, CC) in cast_jobs:
                for r0 in range(0, RR, 128):
                    for c0 in range(0, CC, 2048):
                        cw = min(2048, CC - c0)
                        bi = wi % 4
                        eng = "gpsimd" if wi % 2 == 0 else "vector"
                        if bi in wc_mark:
                            S.wait("sync", c_g if wc_mark[bi][0] == "gpsimd" else c_v,
                                   wc_mark[bi][1])
                        S.dma("sync", c_ld, cstg[bi][:, 0:cw], srcT[r0:r0 + 128, c0:c0 + cw])
                        S.wait(eng, c_ld)
                        if bi in wst_mark:
                            S.wait(eng, c_st, wst_mark[bi])
                        S.op(eng, lambda e, bi=bi, cw=cw: e.tensor_copy(
                            out=cbf[bi][:, 0:cw], in_=cstg[bi][:, 0:cw]))
                        mk = S.fence(eng, c_g if eng == "gpsimd" else c_v)
                        wc_mark[bi] = (eng, mk)
                        S.wait("scalar", c_g if eng == "gpsimd" else c_v, mk)
                        wst_mark[bi] = S.dma("scalar", c_st, dstT[r0:r0 + 128, c0:c0 + cw],
                                             cbf[bi][:, 0:cw])
                        wi += 1
                cast_done[jname] = c_st.v
            w_cast_done = c_st.v

            S.wait("scalar", c_ld, ld_pre)
            S.op("scalar", lambda e: e.activation(out=s_negA[:], in_=s_negA[:], func=AF.Exp))
            sm = S.fence("scalar", c_s)
            S.wait("vector", c_s, sm)
            S.op("vector", lambda e: e.tensor_scalar(out=s_negA[:], in0=s_negA[:], scalar1=-1.0,
                                                     scalar2=None, op0=OP.mult))
            S.op("vector", lambda e: e.memset(carry[:], 0.0))
            S.op("vector", lambda e: e.memset(convh[:], 0.0))
            v_prol = S.fence("vector", c_v)

            p1m = []
            for p in range(NPAN):
                if p >= 2:
                    S.wait("sync", c_s, p1m[p - 2])
                S.dma("sync", c_ld, xpan[p % 2][:], d_xin[128 * p:128 * (p + 1), :])
                S.wait("scalar", c_ld)
                xb = xpan[p % 2]
                S.op("scalar", lambda e, xb=xb, p=p: e.activation(
                    out=scr1[:], in_=xb[:], func=AF.Identity, accum_out=stats[:, 4 * p:4 * p + 1]))
                S.op("scalar", lambda e, xb=xb, p=p: e.activation(
                    out=scr1[:], in_=xb[:], func=AF.Square, accum_out=stats[:, 4 * p + 1:4 * p + 2]))
                p1m.append(S.fence("scalar", c_s))

            st3 = stats[:].rearrange("p (a b) -> p a b", b=4)
            mr3 = mrs[:].rearrange("p (a b) -> p a b", b=4)
            ty3 = tiny[:].rearrange("p (a b) -> p a b", b=8)

            def stat_math(i_sum, i_sq, o_mu, o_rstd):
                S.op("vector", lambda e: e.tensor_scalar(
                    out=mr3[:, :, o_mu], in0=st3[:, :, i_sum], scalar1=1.0 / H, scalar2=None,
                    op0=OP.mult))
                S.op("vector", lambda e: e.tensor_scalar(
                    out=ty3[:, :, 0], in0=st3[:, :, i_sq], scalar1=1.0 / H, scalar2=None,
                    op0=OP.mult))
                S.op("vector", lambda e: e.drain())
                S.op("vector", lambda e: e.tensor_tensor(
                    out=ty3[:, :, 1], in0=mr3[:, :, o_mu], in1=mr3[:, :, o_mu], op=OP.mult))
                S.op("vector", lambda e: e.drain())
                S.op("vector", lambda e: e.tensor_tensor(
                    out=ty3[:, :, 2], in0=ty3[:, :, 0], in1=ty3[:, :, 1], op=OP.subtract))
                S.op("vector", lambda e: e.drain())
                S.op("vector", lambda e: e.tensor_scalar(
                    out=ty3[:, :, 2], in0=ty3[:, :, 2], scalar1=1e-5, scalar2=None, op0=OP.add))
                S.op("vector", lambda e: e.drain())
                vm = S.fence("vector", c_v)
                S.wait("scalar", c_v, vm)
                S.op("scalar", lambda e: e.activation(out=ty3[:, :, 3], in_=ty3[:, :, 2],
                                                      func=AF.Sqrt))
                sm2 = S.fence("scalar", c_s)
                S.wait("vector", c_s, sm2)
                S.op("vector", lambda e: e.reciprocal(out=mr3[:, :, o_rstd], in_=ty3[:, :, 3]))
                S.op("vector", lambda e: e.drain())
                return S.fence("vector", c_v)

            S.wait("vector", c_s, p1m[-1])
            S.wait("vector", c_v, v_prol)
            v_ln1 = stat_math(0, 1, 0, 1)

            p2v, p2s = [], []
            for p in range(NPAN):
                if p < 2:
                    S.wait("sync", c_s, p1m[-1])
                if p >= 2:
                    S.wait("sync", c_v, p2v[p - 2])
                S.dma("sync", c_ld, xpan[p % 2][:], d_xin[128 * p:128 * (p + 1), :])
                S.wait("vector", c_ld)
                if p == 0:
                    S.wait("vector", c_v, v_ln1)
                else:
                    S.wait("vector", c_s, p2s[p - 1])
                xb = xpan[p % 2]
                S.op("vector", lambda e, xb=xb, p=p: e.tensor_scalar(
                    out=scr2[:], in0=xb[:], scalar1=mr3[:, p:p + 1, 0], scalar2=mr3[:, p:p + 1, 1],
                    op0=OP.subtract, op1=OP.mult))
                vm = S.fence("vector", c_v)
                p2v.append(vm)
                S.wait("scalar", c_v, vm)
                S.op("scalar", lambda e, p=p: e.activation(
                    out=scr1[:], in_=scr2[:], func=AF.Identity,
                    accum_out=stats[:, 4 * p + 2:4 * p + 3]))
                S.op("scalar", lambda e, p=p: e.activation(
                    out=scr1[:], in_=scr2[:], func=AF.Square,
                    accum_out=stats[:, 4 * p + 3:4 * p + 4]))
                p2s.append(S.fence("scalar", c_s))
            S.wait("vector", c_s, p2s[-1])
            v_ln2 = stat_math(2, 3, 2, 3)

            zv = []
            for p in range(NPAN):
                if p < 2:
                    S.wait("sync", c_s, p2s[-1])
                    S.wait("sync", c_v, p2v[-1])
                if p >= 2:
                    S.wait("sync", c_v, zv[p - 2])
                S.dma("sync", c_ld, xpan[p % 2][:], d_xin[128 * p:128 * (p + 1), :])
                S.wait("vector", c_ld)
                if p == 0:
                    S.wait("vector", c_v, v_ln2)
                xb = xpan[p % 2]
                S.op("vector", lambda e, xb=xb, p=p: e.tensor_scalar(
                    out=scr2[:], in0=xb[:], scalar1=mr3[:, p:p + 1, 0], scalar2=mr3[:, p:p + 1, 1],
                    op0=OP.subtract, op1=OP.mult))
                S.op("vector", lambda e: e.drain())
                S.op("vector", lambda e, p=p: e.tensor_scalar(
                    out=zpan[:], in0=scr2[:], scalar1=mr3[:, p:p + 1, 2], scalar2=mr3[:, p:p + 1, 3],
                    op0=OP.subtract, op1=OP.mult))
                vm = S.fence("vector", c_v)
                zv.append(vm)
                S.wait("tensor", c_v, vm)
                for c in range(8):
                    S.op("tensor", lambda e, c=c: e.transpose(
                        psT[:, 128 * c:128 * (c + 1)], zpan[:, 128 * c:128 * (c + 1)], s_eyeb[:]))
                tm = S.fence("tensor", c_t)
                S.wait("vector", c_t, tm)
                for c in range(8):
                    S.op("vector", lambda e, c=c, p=p: e.tensor_copy(
                        out=zT[c][:, 128 * p:128 * (p + 1)], in_=psT[:, 128 * c:128 * (c + 1)]))
                vm2 = S.fence("vector", c_v)
                S.wait("tensor", c_v, vm2)
            if debug:
                vz = S.fence("vector", c_v)
                S.wait("gpsimd", c_v, vz)
                for c in range(8):
                    S.dma("gpsimd", c_dbg, dbg["z"][128 * c:128 * (c + 1), :], zT[c][:])
                S.dma("gpsimd", c_dbg, dbg["stats"][:, 0:4 * NPAN], stats[:])
                S.dma("gpsimd", c_dbg, dbg["stats"][:, 4 * NPAN:8 * NPAN], mrs[:])
                S.op("gpsimd", lambda e: e.drain())
            barrier()

        with contextlib.ExitStack() as ctx1:
            sb1 = mk_alloc(ctx1)
            normT = [sb1(f"normT{c}", [128, 512], BF16) for c in range(8)]
            hff = [sb1(f"hff{j}", [128, 512], BF16) for j in range(24)]
            gacc = [sb1(f"gacc{c}", [128, T], BF16) for c in range(8)]
            wbfg = [sb1(f"wbfg{i}", [128, 1024], BF16) for i in range(4)]
            wdnb = [sb1(f"wdnb{i}", [128, 3072], BF16) for i in range(2)]
            cbc = [sb1(f"cbc{i}", [128, 512], BF16) for i in range(2)]
            crsc = [sb1(f"crsc{i}", [1, 512], F32) for i in range(2)]
            sgb = [sb1(f"sgb{i}", [128, 512], BF16) for i in range(2)]
            gsc = sb1("gsc", [128, 512], F32)
            upe = [sb1(f"upe{i}", [128, 512], BF16) for i in range(2)]
            logits_sb = sb1("logits_sb", [8, 512], F32)
            rsc = sb1("rsc", [128, 64], F32)
            rmsk = sb1("rmsk", [128, 16], U8)
            combs = [sb1(f"comb{p}", [128, 8], F32) for p in range(16)]
            combrow = sb1("combrow", [8, T], F32)
            s_eyef = sb1("s_eyef", [128, 128], F32)
            S.dma("sync", c_ld, s_eyef[:], d_eyef)
            S.wait("tensor", c_ld)
            gtp = [sb1(f"gtp{i}", [128, H], BF16) for i in range(2)]

            for ti, (t0, tw) in enumerate(FTS):
                for c in range(8):
                    S.op("vector", lambda e, c=c, t0=t0, tw=tw: e.tensor_scalar(
                        out=normT[c][:, 0:tw], in0=zT[c][:, t0:t0 + tw],
                        scalar1=s_gam[:, c:c + 1], scalar2=s_bet[:, c:c + 1],
                        op0=OP.mult, op1=OP.add))
                vm = S.fence("vector", c_v)
                S.wait("tensor", c_v, vm)
                for c in range(8):
                    S.op("tensor", lambda e, c=c, tw=tw: e.matmul(
                        psR[0:8, 0:tw], s_rT[c][:], normT[c][:, 0:tw],
                        start=(c == 0), stop=(c == 7)))
                tm = S.fence("tensor", c_t)
                S.wait("vector", c_t, tm)
                S.op("vector", lambda e, tw=tw: e.tensor_copy(out=logits_sb[:, 0:tw],
                                                              in_=psR[0:8, 0:tw]))
                v_lg = S.fence("vector", c_v)
                for q in range(tw // 128):
                    pan = ti * 4 + q
                    S.wait("tensor", c_v, v_lg)
                    S.op("tensor", lambda e, q=q: e.transpose(
                        psTf[:, 0:8], logits_sb[:, 128 * q:128 * (q + 1)], s_eyef[0:8, 0:8]))
                    tm2 = S.fence("tensor", c_t)
                    S.wait("vector", c_t, tm2)
                    S.op("vector", lambda e: e.tensor_copy(out=rsc[:, 0:8], in_=psTf[:, 0:8]))
                    vq = S.fence("vector", c_v)
                    S.wait("tensor", c_v, vq)
                    S.op("vector", lambda e: e.tensor_reduce(out=rsc[:, 8:9], in_=rsc[:, 0:8],
                                                             axis=AX.X, op=OP.max, negate=True))
                    S.op("vector", lambda e: e.drain())
                    vq2 = S.fence("vector", c_v)
                    S.wait("scalar", c_v, vq2)
                    S.op("scalar", lambda e: e.activation(out=rsc[:, 16:24], in_=rsc[:, 0:8],
                                                          func=AF.Exp, bias=rsc[:, 8:9]))
                    sq = S.fence("scalar", c_s)
                    S.wait("vector", c_s, sq)
                    S.op("vector", lambda e: e.tensor_reduce(out=rsc[:, 9:10], in_=rsc[:, 16:24],
                                                             axis=AX.X, op=OP.add))
                    S.op("vector", lambda e: e.drain())
                    S.op("vector", lambda e: e.reciprocal(out=rsc[:, 10:11], in_=rsc[:, 9:10]))
                    S.op("vector", lambda e: e.drain())
                    S.op("vector", lambda e: e.tensor_scalar(
                        out=rsc[:, 24:32], in0=rsc[:, 16:24], scalar1=rsc[:, 10:11], scalar2=None,
                        op0=OP.mult))
                    S.op("vector", lambda e: e.drain())
                    S.op("vector", lambda e: e.tensor_reduce(out=rsc[:, 11:12], in_=rsc[:, 24:32],
                                                             axis=AX.X, op=OP.max))
                    S.op("vector", lambda e: e.drain())
                    S.op("vector", lambda e: e.tensor_scalar(
                        out=rmsk[:, 0:8], in0=rsc[:, 24:32], scalar1=rsc[:, 11:12], scalar2=None,
                        op0=OP.is_ge))
                    S.op("vector", lambda e: e.drain())
                    S.op("vector", lambda e: e.tensor_copy(out=rsc[:, 32:40], in_=rmsk[:, 0:8]))
                    S.op("vector", lambda e: e.drain())
                    S.op("vector", lambda e: e.scalar_tensor_tensor(
                        out=rsc[:, 40:48], in0=rsc[:, 32:40], scalar=-1e30, in1=rsc[:, 24:32],
                        op0=OP.mult, op1=OP.add))
                    S.op("vector", lambda e: e.drain())
                    S.op("vector", lambda e: e.tensor_reduce(out=rsc[:, 12:13], in_=rsc[:, 40:48],
                                                             axis=AX.X, op=OP.max))
                    S.op("vector", lambda e: e.drain())
                    S.op("vector", lambda e: e.tensor_scalar(
                        out=rmsk[:, 8:16], in0=rsc[:, 40:48], scalar1=rsc[:, 12:13], scalar2=None,
                        op0=OP.is_ge))
                    S.op("vector", lambda e: e.drain())
                    S.op("vector", lambda e: e.tensor_copy(out=rsc[:, 48:56], in_=rmsk[:, 8:16]))
                    S.op("vector", lambda e: e.drain())
                    vq3 = S.fence("vector", c_v)
                    S.wait("scalar", c_v, vq3)
                    S.op("scalar", lambda e: e.activation(out=rsc[:, 13:14], in_=rsc[:, 11:12],
                                                          func=AF.Exp))
                    S.op("scalar", lambda e: e.activation(out=rsc[:, 14:15], in_=rsc[:, 12:13],
                                                          func=AF.Exp))
                    sq2 = S.fence("scalar", c_s)
                    S.wait("vector", c_s, sq2)
                    S.op("vector", lambda e: e.tensor_tensor(out=rsc[:, 15:16], in0=rsc[:, 13:14],
                                                             in1=rsc[:, 14:15], op=OP.add))
                    S.op("vector", lambda e: e.drain())
                    S.op("vector", lambda e: e.reciprocal(out=rsc[:, 56:57], in_=rsc[:, 15:16]))
                    S.op("vector", lambda e: e.drain())
                    S.op("vector", lambda e: e.tensor_scalar(
                        out=rsc[:, 13:14], in0=rsc[:, 13:14], scalar1=rsc[:, 56:57], scalar2=None,
                        op0=OP.mult))
                    S.op("vector", lambda e: e.tensor_scalar(
                        out=rsc[:, 14:15], in0=rsc[:, 14:15], scalar1=rsc[:, 56:57], scalar2=None,
                        op0=OP.mult))
                    S.op("vector", lambda e: e.drain())
                    S.op("vector", lambda e, pan=pan: e.tensor_scalar(
                        out=combs[pan][:], in0=rsc[:, 32:40], scalar1=rsc[:, 13:14], scalar2=None,
                        op0=OP.mult))
                    S.op("vector", lambda e: e.drain())
                    S.op("vector", lambda e, pan=pan: e.scalar_tensor_tensor(
                        out=combs[pan][:], in0=rsc[:, 48:56], scalar=rsc[:, 14:15],
                        in1=combs[pan][:], op0=OP.mult, op1=OP.add))
                    S.op("vector", lambda e: e.drain())
            v_comb = S.fence("vector", c_v)
            if debug:
                S.wait("gpsimd", c_v, v_comb)
                for pan in range(16):
                    S.dma("gpsimd", c_dbg, dbg["comb"][128 * pan:128 * (pan + 1), :], combs[pan][:])
                S.op("gpsimd", lambda e: e.drain())
            for pan in range(16):
                S.wait("tensor", c_v, v_comb)
                S.op("tensor", lambda e, pan=pan: e.transpose(psTf[0:8, :], combs[pan][:],
                                                              s_eyef[:]))
                tm = S.fence("tensor", c_t)
                S.wait("vector", c_t, tm)
                S.op("vector", lambda e, pan=pan: e.tensor_copy(
                    out=combrow[:, 128 * pan:128 * (pan + 1)], in_=psTf[0:8, :]))
                vm = S.fence("vector", c_v)
                S.wait("tensor", c_v, vm)
            v_combrow = c_v.v

            def gu_block(e_, j, half):
                f0 = j * 128 + half * DFF
                return b_guT[e_].rearrange("(k p) f -> p k f", p=128)[:, :, f0:f0 + 128]

            def sgu_block(j, half):
                f0 = j * 128 + half * SFF
                return b_sguT.rearrange("(k p) f -> p k f", p=128)[:, :, f0:f0 + 128]

            last_pair_mm = 0
            last_dn_mm = 0
            cbc_mark = {}
            crsc_mark = {}
            dstg_mark = {}
            dwbf_mark = {}
            for e_ in range(NEXP + 1):
                shared = (e_ == NEXP)
                npair = 8 if shared else 24
                for ti, (t0, tw) in enumerate(FTS):
                    S.wait("vector", c_t, last_pair_mm)
                    for c in range(8):
                        S.op("vector", lambda e, c=c, t0=t0, tw=tw: e.tensor_scalar(
                            out=normT[c][:, 0:tw], in0=zT[c][:, t0:t0 + tw],
                            scalar1=s_gam[:, c:c + 1], scalar2=s_bet[:, c:c + 1],
                            op0=OP.mult, op1=OP.add))
                    v_norm = S.fence("vector", c_v)
                    if not shared:
                        S.wait("scalar", c_v, v_combrow)
                        S.wait("scalar", c_t, crsc_mark.get(ti % 2, 0))
                        S.dma("scalar", c_row, crsc[ti % 2][:, 0:tw],
                              combrow[e_:e_ + 1, t0 - HALO:t0 - HALO + tw])
                        mk_cr = c_row.v
                        S.wait("tensor", c_row, mk_cr)
                        S.wait("tensor", c_v, cbc_mark.get(ti % 2, v_combrow))
                        S.op("tensor", lambda e, ti=ti, tw=tw: e.matmul(
                            psR[:, 0:tw], s_ones[:], crsc[ti % 2][:, 0:tw],
                            start=True, stop=True))
                        tmc = S.fence("tensor", c_t)
                        crsc_mark[ti % 2] = tmc
                        S.wait("vector", c_t, tmc)
                        S.op("vector", lambda e, ti=ti, tw=tw: e.tensor_copy(
                            out=cbc[ti % 2][:, 0:tw], in_=psR[:, 0:tw]))
                        vmc = S.fence("vector", c_v)
                        cbc_mark[ti % 2] = vmc
                        S.wait("tensor", c_v, vmc)
                    pair_v = {}
                    pair_s = {}
                    if ti == 0:
                        S.wait("sync", c_st,
                               cast_done["sgu"] if shared else cast_done[f"gu{e_}"])
                        S.wait("sync", c_st,
                               cast_done["sdn"] if shared else cast_done[f"dn{e_}"])
                    for j in range(npair):
                        for half in (0, 1):
                            bi = (2 * j + half) % 4
                            S.wait("sync", c_t, wbf_mark.get(bi, 0))
                            src = sgu_block(j, half) if shared else gu_block(e_, j, half)
                            S.dma("sync", c_ld, wbfg[bi][:].rearrange("p (k f) -> p k f", k=8), src)
                            stg_mark[bi] = c_ld.v
                        S.wait("tensor", c_ld, stg_mark[(2 * j + 1) % 4])
                        if j == 0:
                            S.wait("tensor", c_v, v_norm)
                        if j >= 2:
                            S.wait("tensor", c_v, pair_v[j - 2])
                            S.wait("tensor", c_s, pair_s[j - 2])
                        bkg, bku = (psA, psB) if j % 2 == 0 else (psC, psD)
                        for half, bank in ((0, bkg), (1, bku)):
                            bi = (2 * j + half) % 4
                            for k in range(8):
                                S.op("tensor", lambda e, bi=bi, k=k, bank=bank, tw=tw: e.matmul(
                                    bank[:, 0:tw], wbfg[bi][:, 128 * k:128 * (k + 1)],
                                    normT[k][:, 0:tw], start=(k == 0), stop=(k == 7)))
                        tmm = S.fence("tensor", c_t)
                        wbf_mark[(2 * j) % 4] = tmm
                        wbf_mark[(2 * j + 1) % 4] = tmm
                        last_pair_mm = tmm
                        S.wait("scalar", c_t, tmm)
                        if j >= 2:
                            S.wait("scalar", c_v, pair_v[j - 2])
                        S.op("scalar", lambda e, j=j, tw=tw, bkg=bkg: e.activation(
                            out=sgb[j % 2][:, 0:tw], in_=bkg[:, 0:tw], func=AF.Silu))
                        smk = S.fence("scalar", c_s)
                        pair_s[j] = smk
                        S.wait("vector", c_t, tmm)
                        S.op("vector", lambda e, j=j, tw=tw, bku=bku: e.tensor_copy(
                            out=upe[j % 2][:, 0:tw], in_=bku[:, 0:tw]))
                        S.wait("vector", c_s, smk)
                        S.wait("vector", c_t, last_dn_mm)
                        S.op("vector", lambda e, j=j, tw=tw: e.tensor_tensor(
                            out=hff[j][:, 0:tw], in0=sgb[j % 2][:, 0:tw], in1=upe[j % 2][:, 0:tw],
                            op=OP.mult))
                        pair_v[j] = S.fence("vector", c_v)
                    v_hff = pair_v[npair - 1]
                    dn_ev = {}
                    for c in range(8):
                        di = c % 2
                        S.wait("sync", c_t, dwbf_mark.get(di, 0))
                        src = (b_sdnT if shared else b_dnT[e_]).rearrange(
                            "(k p) f -> p k f", p=128)[:, 0:npair, 128 * c:128 * (c + 1)]
                        S.dma("sync", c_ld,
                              wdnb[di][:, 0:128 * npair].rearrange("p (k f) -> p k f", k=npair),
                              src)
                        dstg_mark[di] = c_ld.v
                        S.wait("tensor", c_ld, dstg_mark[di])
                        if c == 0:
                            S.wait("tensor", c_v, v_hff)
                        if c >= 2:
                            S.wait("tensor", c_v, dn_ev[c - 2])
                        bank = psD if c % 2 == 0 else psE
                        for k in range(npair):
                            S.op("tensor", lambda e, di=di, k=k, bank=bank, tw=tw: e.matmul(
                                bank[:, 0:tw], wdnb[di][:, 128 * k:128 * (k + 1)], hff[k][:, 0:tw],
                                start=(k == 0), stop=(k == npair - 1)))
                        tdn = S.fence("tensor", c_t)
                        dwbf_mark[di] = tdn
                        last_dn_mm = tdn
                        S.wait("vector", c_t, tdn)
                        if e_ == 0:
                            S.op("vector", lambda e, c=c, t0=t0, ti=ti, tw=tw, bank=bank:
                                 e.tensor_tensor(
                                     out=gacc[c][:, t0 - HALO:t0 - HALO + tw],
                                     in0=bank[:, 0:tw], in1=cbc[ti % 2][:, 0:tw], op=OP.mult))
                        elif not shared:
                            S.op("vector", lambda e, ti=ti, tw=tw, bank=bank: e.tensor_tensor(
                                out=gsc[:, 0:tw], in0=bank[:, 0:tw], in1=cbc[ti % 2][:, 0:tw],
                                op=OP.mult))
                            S.op("vector", lambda e, c=c, t0=t0, tw=tw: e.tensor_tensor(
                                out=gacc[c][:, t0 - HALO:t0 - HALO + tw], in0=gsc[:, 0:tw],
                                in1=gacc[c][:, t0 - HALO:t0 - HALO + tw], op=OP.add))
                        else:
                            S.op("vector", lambda e, c=c, t0=t0, tw=tw, bank=bank: e.tensor_tensor(
                                out=gacc[c][:, t0 - HALO:t0 - HALO + tw], in0=bank[:, 0:tw],
                                in1=gacc[c][:, t0 - HALO:t0 - HALO + tw], op=OP.add))
                        dn_ev[c] = S.fence("vector", c_v)
            v_ffn = c_v.v

            gst = []
            for pan in range(16):
                S.wait("tensor", c_v, v_ffn)
                if pan >= 2:
                    S.wait("vector", c_st, gst[pan - 2])
                for c in range(8):
                    S.op("tensor", lambda e, c=c, pan=pan: e.transpose(
                        psT[:, 128 * c:128 * (c + 1)], gacc[c][:, 128 * pan:128 * (pan + 1)],
                        s_eyeb[:]))
                tm = S.fence("tensor", c_t)
                S.wait("vector", c_t, tm)
                S.op("vector", lambda e, pan=pan: e.tensor_copy(out=gtp[pan % 2][:], in_=psT[:]))
                vm = S.fence("vector", c_v)
                S.wait("tensor", c_v, vm)
                S.wait("gpsimd", c_v, vm)
                gst.append(S.dma("gpsimd", c_st, d_gsp[128 * pan:128 * (pan + 1), :],
                                 gtp[pan % 2][:]))
            if debug:
                S.wait("gpsimd", c_st, c_st.v)
                for pan in range(16):
                    S.dma("gpsimd", c_dbg, dbg["g"][128 * pan:128 * (pan + 1), :],
                          d_gsp[128 * pan:128 * (pan + 1), :])
                S.op("gpsimd", lambda e: e.drain())
            barrier()
        g_spilled = c_st.v
        stg_mark = {}
        wbf_mark = {}

        with contextlib.ExitStack() as ctx2:
            sb2 = mk_alloc(ctx2)
            xs_c = [sb2(f"xs{c}", [128, OTW], BF16) for c in range(16)]
            gt_c = [sb2(f"gt{c}", [128, OTW], BF16) for c in range(16)]
            ym_c = [sb2(f"ym{c}", [128, OTW], BF16) for c in range(16)]
            dtb = [sb2(f"dtb{c}", [128, OTW], BF16) for c in range(16)]
            bbc = [sb2(f"bbc{n}", [128, OTW], BF16) for n in range(16)]
            ccb = [sb2(f"ccb{n}", [128, OTW], BF16) for n in range(16)]
            params = sb2("params", [128, OTW], F32)
            brow_t = [sb2("brow0", [1, OTW], F32)]
            crow_t = [sb2("crow0", [1, OTW], F32)]
            u_t = [sb2(f"u{i}", [128, OTW], BF16) for i in range(2)]
            dec_t = [sb2(f"dec{i}", [128, OTW], BF16) for i in range(4)]
            uu_t = [sb2(f"uu{i}", [128, OTW], BF16) for i in range(4)]
            ss_t = [sb2(f"ss{i}", [128, OTW], F32) for i in range(4)]
            sm_t = [sb2(f"sm{i}", [128, OTW], BF16) for i in range(4)]
            cscr = [sb2(f"cscr{i}", [128, OTW], F32) for i in range(2)]
            xm_c = [sb2(f"xm{i}", [128, OTW + 3], F32) for i in range(2)]
            mbfg = [sb2(f"mbfg{i}", [128, 1024], BF16) for i in range(4)]
            spre = bbc[8:16]
            hnT = bbc[0:8]
            gtp2 = [sb2("gtp2_0", [128, H], BF16)]
            xfin = [sb2("xfin0", [128, H], F32)]
            xnf = sb2("xnf", [128, H], BF16)
            sfin = sb2("sfin", [128, H], BF16)
            ofin = [sb2("ofin0", [128, H], F32)]

            def win_block(f0):
                return b_winT.rearrange("(k p) f -> p k f", p=128)[:, :, f0:f0 + 128]

            def out_block(h0, klo):
                return b_outT.rearrange("(k p) f -> p k f", p=128)[:, klo:klo + 8,
                                                                   128 * h0:128 * (h0 + 1)]

            last_ip_mm = 0
            last_grp_g = 0
            dbg_mark = 0
            fin_s = {}
            fin_v = {}
            fin_st = {}
            gld_v = {}
            first_fin = True
            for oi, (t0, tw) in enumerate(OTS):
                is_halo = (oi == 0)
                S.wait("vector", c_t, last_ip_mm)
                if debug:
                    S.wait("vector", c_g, dbg_mark)
                    S.wait("scalar", c_g, dbg_mark)
                for c in range(8):
                    S.op("vector", lambda e, c=c, t0=t0, tw=tw: e.tensor_scalar(
                        out=hnT[c][:, 0:tw], in0=zT[c][:, t0:t0 + tw],
                        scalar1=s_lnw[:, c:c + 1], scalar2=s_lnb[:, c:c + 1],
                        op0=OP.mult, op1=OP.add))
                v_hn = S.fence("vector", c_v)
                ipev = {}
                for f in range(32):
                    bi = f % 2
                    if oi == 0 and f == 0:
                        S.wait("sync", c_st, cast_done["win"])
                        S.wait("sync", c_st, cast_done["out"])
                    S.wait("sync", c_t, wbf_mark.get(bi, 0))
                    S.dma("sync", c_ld, mbfg[bi][:].rearrange("p (k f) -> p k f", k=8),
                          win_block(128 * f))
                    stg_mark[bi] = c_ld.v
                    S.wait("tensor", c_ld, stg_mark[bi])
                    if f == 0:
                        S.wait("tensor", c_v, v_hn)
                    if f >= 2:
                        S.wait("tensor", c_v, ipev[f - 2])
                    bank = psA if f % 2 == 0 else psB
                    for k in range(8):
                        S.op("tensor", lambda e, bi=bi, k=k, bank=bank, tw=tw: e.matmul(
                            bank[:, 0:tw], mbfg[bi][:, 128 * k:128 * (k + 1)], hnT[k][:, 0:tw],
                            start=(k == 0), stop=(k == 7)))
                    tmm = S.fence("tensor", c_t)
                    wbf_mark[bi] = tmm
                    last_ip_mm = tmm
                    S.wait("scalar", c_t, tmm)
                    if f < 16:
                        ci = f
                        S.op("scalar", lambda e, ci=ci, tw=tw, bank=bank: e.activation(
                            out=xm_c[ci % 2][:, 3:3 + tw], in_=bank[:, 0:tw], func=AF.Identity,
                            bias=s_inb[:, ci:ci + 1]))
                        smk = S.fence("scalar", c_s)
                        S.wait("vector", c_s, smk)
                        S.op("vector", lambda e, ci=ci: e.tensor_copy(
                            out=xm_c[ci % 2][:, 0:3], in_=convh[:, 3 * ci:3 * ci + 3]))
                        if is_halo:
                            S.op("vector", lambda e, ci=ci, tw=tw: e.tensor_scalar(
                                out=xm_c[ci % 2][:, 3:3 + tw], in0=xm_c[ci % 2][:, 3:3 + tw],
                                scalar1=s_umask[:], scalar2=None, op0=OP.mult))
                        S.op("vector", lambda e: e.drain())
                        S.op("vector", lambda e, ci=ci, tw=tw: e.tensor_copy(
                            out=convh[:, 3 * ci:3 * ci + 3], in_=xm_c[ci % 2][:, tw:tw + 3]))
                        if ci >= 2:
                            S.wait("vector", c_s, conv_s[ci - 2])
                        S.op("vector", lambda e, ci=ci, tw=tw: e.tensor_scalar(
                            out=cscr[ci % 2][:, 0:tw], in0=xm_c[ci % 2][:, 0:tw],
                            scalar1=s_convw[ci][:, 0:1], scalar2=None, op0=OP.mult))
                        for kk in range(1, 4):
                            S.op("vector", lambda e, ci=ci, tw=tw, kk=kk: e.scalar_tensor_tensor(
                                out=cscr[ci % 2][:, 0:tw], in0=xm_c[ci % 2][:, kk:kk + tw],
                                scalar=s_convw[ci][:, kk:kk + 1], in1=cscr[ci % 2][:, 0:tw],
                                op0=OP.mult, op1=OP.add))
                        vmk = S.fence("vector", c_v)
                        S.wait("scalar", c_v, vmk)
                        S.op("scalar", lambda e, ci=ci, tw=tw: e.activation(
                            out=xs_c[ci][:, 0:tw], in_=cscr[ci % 2][:, 0:tw], func=AF.Silu,
                            bias=s_convb[:, ci:ci + 1]))
                        if ci == 0:
                            conv_s = {}
                        conv_s[ci] = S.fence("scalar", c_s)
                        ipev[f] = S.fence("vector", c_v)
                    else:
                        ci = f - 16
                        S.op("scalar", lambda e, ci=ci, tw=tw, bank=bank: e.activation(
                            out=gt_c[ci][:, 0:tw], in_=bank[:, 0:tw], func=AF.Identity,
                            bias=s_inb[:, 16 + ci:17 + ci]))
                        smk = S.fence("scalar", c_s)
                        S.wait("vector", c_s, smk)
                        ipev[f] = S.fence("vector", c_v)
                v_xs = c_v.v
                s_xs = c_s.v
                if debug and not is_halo:
                    S.wait("gpsimd", c_s, s_xs)
                    S.wait("gpsimd", c_v, v_xs)
                    for c in range(16):
                        S.dma("gpsimd", c_dbg, dbg["xs"][128 * c:128 * (c + 1), t0:t0 + tw],
                              xs_c[c][:, 0:tw])
                    dbg_mark = S.fence("gpsimd", c_g)
                if not is_halo:
                    for ci in range(16):
                        S.op("scalar", lambda e, ci=ci, tw=tw: e.activation(
                            out=gt_c[ci][:, 0:tw], in_=gt_c[ci][:, 0:tw], func=AF.Silu))
                s_sgy = S.fence("scalar", c_s)
                S.wait("tensor", c_s, s_xs)
                S.wait("tensor", c_v, v_xs)
                for k in range(16):
                    S.op("tensor", lambda e, k=k, tw=tw: e.matmul(
                        psC[0:96, 0:tw], s_xpT[k][:], xs_c[k][:, 0:tw],
                        start=(k == 0), stop=(k == 15)))
                t_par = S.fence("tensor", c_t)
                S.wait("scalar", c_t, t_par)
                S.op("scalar", lambda e, tw=tw: e.activation(
                    out=params[0:96, 0:tw], in_=psC[0:96, 0:tw], func=AF.Identity))
                s_par = S.fence("scalar", c_s)
                dt_s = {}
                for dhi in range(16):
                    S.wait("tensor", c_s, s_par)
                    if dhi >= 2:
                        S.wait("tensor", c_s, dt_s[dhi - 2])
                    bank = psD if dhi % 2 == 0 else psE
                    S.op("tensor", lambda e, dhi=dhi, tw=tw, bank=bank: e.matmul(
                        bank[:, 0:tw], s_dtpT[:, 128 * dhi:128 * (dhi + 1)], params[0:64, 0:tw],
                        start=True, stop=True))
                    t_dt = S.fence("tensor", c_t)
                    S.wait("scalar", c_t, t_dt)
                    S.op("scalar", lambda e, dhi=dhi, tw=tw, bank=bank: e.activation(
                        out=scr2[:, 0:tw], in_=bank[:, 0:tw], func=AF.Exp,
                        bias=s_dtpb[:, dhi:dhi + 1]))
                    S.op("scalar", lambda e, dhi=dhi, tw=tw: e.activation(
                        out=dtb[dhi][:, 0:tw], in_=scr2[:, 0:tw], func=AF.Ln, bias=1.0))
                    dt_s[dhi] = S.fence("scalar", c_s)
                s_dtall = dt_s[15]
                if debug and not is_halo:
                    S.wait("gpsimd", c_s, s_dtall)
                    for c in range(16):
                        S.dma("gpsimd", c_dbg, dbg["dt"][128 * c:128 * (c + 1), t0:t0 + tw],
                              dtb[c][:, 0:tw])
                    dbg_mark = S.fence("gpsimd", c_g)
                bc_v = {}
                for n in range(NST):
                    if n >= 1:
                        S.wait("scalar", c_t, bc_t)
                    S.dma("scalar", c_row, brow_t[0][:, 0:tw], params[64 + n:65 + n, 0:tw])
                    S.dma("scalar", c_row, crow_t[0][:, 0:tw], params[80 + n:81 + n, 0:tw])
                    S.wait("tensor", c_row)
                    if n >= 1:
                        S.wait("tensor", c_v, bc_v[n - 1])
                    S.op("tensor", lambda e, n=n, tw=tw: e.matmul(
                        psR[:, 0:tw], s_ones[:], brow_t[0][:, 0:tw], start=True, stop=True))
                    S.op("tensor", lambda e, n=n, tw=tw: e.matmul(
                        psC[:, 0:tw], s_ones[:], crow_t[0][:, 0:tw], start=True, stop=True))
                    bc_t = S.fence("tensor", c_t)
                    S.wait("vector", c_t, bc_t)
                    S.op("vector", lambda e, n=n, tw=tw: e.tensor_copy(
                        out=bbc[n][:, 0:tw], in_=psR[:, 0:tw]))
                    S.op("vector", lambda e, n=n, tw=tw: e.tensor_copy(
                        out=ccb[n][:, 0:tw], in_=psC[:, 0:tw]))
                    bc_v[n] = S.fence("vector", c_v)
                grp_v = {}
                grp_g = {}
                for dhi in range(16):
                    S.wait("vector", c_s, s_dtall)
                    S.op("vector", lambda e, dhi=dhi, tw=tw: e.tensor_tensor(
                        out=u_t[dhi % 2][:, 0:tw], in0=dtb[dhi][:, 0:tw],
                        in1=xs_c[dhi][:, 0:tw], op=OP.mult))
                    if is_halo:
                        S.op("vector", lambda e, dhi=dhi, tw=tw: e.tensor_scalar(
                            out=u_t[dhi % 2][:, 0:tw], in0=u_t[dhi % 2][:, 0:tw],
                            scalar1=s_umask[:], scalar2=None, op0=OP.mult))
                    v_u = S.fence("vector", c_v)
                    def yadd(gy, tw=tw):
                        dh2, n2 = gy // NST, gy % NST
                        S.wait("vector", c_g, grp_g[gy])
                        if n2 == 0:
                            S.op("vector", lambda e, dh2=dh2, n2=n2, tw=tw: e.tensor_copy(
                                out=ym_c[dh2][:, 0:tw], in_=sm_t[n2 % 4][:, 0:tw]))
                        else:
                            S.op("vector", lambda e, dh2=dh2, n2=n2, tw=tw: e.tensor_tensor(
                                out=ym_c[dh2][:, 0:tw], in0=sm_t[n2 % 4][:, 0:tw],
                                in1=ym_c[dh2][:, 0:tw], op=OP.add))
                    for n in range(NST):
                        g = dhi * NST + n
                        if g >= 4:
                            S.wait("scalar", c_v, grp_v[g - 4])
                        elif n == 0 and dhi == 0:
                            S.wait("scalar", c_v, bc_v[NST - 1])
                        S.op("scalar", lambda e, dhi=dhi, n=n, tw=tw: e.activation(
                            out=dec_t[n % 4][:, 0:tw], in_=dtb[dhi][:, 0:tw], func=AF.Exp,
                            scale=s_negA[:, 16 * dhi + n:16 * dhi + n + 1]))
                        s_dec = S.fence("scalar", c_s)
                        S.wait("vector", c_s, s_dec)
                        S.op("vector", lambda e, dhi=dhi, n=n, tw=tw: e.tensor_tensor(
                            out=uu_t[n % 4][:, 0:tw], in0=u_t[dhi % 2][:, 0:tw],
                            in1=bbc[n][:, 0:tw], op=OP.mult))
                        if g >= 4:
                            S.wait("vector", c_g, grp_g[g - 4])
                        elif oi > 0 and g == 0:
                            S.wait("vector", c_g, last_grp_g)
                        S.op("vector", lambda e, g=g, n=n, tw=tw: e.tensor_tensor_scan(
                            out=ss_t[n % 4][:, 0:tw], data0=dec_t[n % 4][:, 0:tw],
                            data1=uu_t[n % 4][:, 0:tw], initial=carry[:, g:g + 1],
                            op0=OP.mult, op1=OP.add))
                        grp_v[g] = S.fence("vector", c_v)
                        if g >= 2:
                            yadd(g - 2)
                        S.wait("gpsimd", c_v, grp_v[g])
                        S.op("gpsimd", lambda e, n=n, tw=tw: e.tensor_tensor(
                            out=sm_t[n % 4][:, 0:tw], in0=ss_t[n % 4][:, 0:tw],
                            in1=ccb[n][:, 0:tw], op=OP.mult))
                        S.op("gpsimd", lambda e, g=g, n=n, tw=tw: e.tensor_copy(
                            out=carry[:, g:g + 1], in_=ss_t[n % 4][:, tw - 1:tw]))
                        grp_g[g] = S.fence("gpsimd", c_g)
                    yadd(dhi * NST + 14)
                    yadd(dhi * NST + 15)
                    S.op("vector", lambda e, dhi=dhi, tw=tw: e.scalar_tensor_tensor(
                        out=ym_c[dhi][:, 0:tw], in0=xs_c[dhi][:, 0:tw],
                        scalar=s_dvec[:, dhi:dhi + 1], in1=ym_c[dhi][:, 0:tw],
                        op0=OP.mult, op1=OP.add))
                    if not is_halo:
                        S.wait("vector", c_s, s_sgy)
                        S.op("vector", lambda e, dhi=dhi, tw=tw: e.tensor_tensor(
                            out=ym_c[dhi][:, 0:tw], in0=ym_c[dhi][:, 0:tw],
                            in1=gt_c[dhi][:, 0:tw], op=OP.mult))
                    S.op("vector", lambda e: e.drain())
                last_grp_g = grp_g[255]
                v_ym = c_v.v
                g_ym = c_g.v
                if debug and not is_halo:
                    S.wait("gpsimd", c_v, v_ym)
                    for c in range(16):
                        S.dma("gpsimd", c_dbg, dbg["ym"][128 * c:128 * (c + 1), t0:t0 + tw],
                              ym_c[c][:, 0:tw])
                    dbg_mark = S.fence("gpsimd", c_g)
                if is_halo:
                    continue
                opev = {}
                for hc in range(8):
                    bi = hc % 2
                    bi2 = 2 + hc % 2
                    S.wait("sync", c_t, wbf_mark.get(bi, 0))
                    S.dma("sync", c_ld, mbfg[bi][:].rearrange("p (k f) -> p k f", k=8),
                          out_block(hc, 0))
                    S.wait("sync", c_t, wbf_mark.get(bi2, 0))
                    S.dma("sync", c_ld, mbfg[bi2][:].rearrange("p (k f) -> p k f", k=8),
                          out_block(hc, 8))
                    stg_mark[bi] = c_ld.v
                    stg_mark[bi2] = c_ld.v
                    S.wait("tensor", c_ld, stg_mark[bi2])
                    if hc == 0:
                        S.wait("tensor", c_v, v_ym)
                        S.wait("tensor", c_g, g_ym)
                    if hc >= 2:
                        S.wait("tensor", c_v, opev[hc - 2])
                    bank = psA if hc % 2 == 0 else psB
                    for k in range(16):
                        wb = mbfg[bi] if k < 8 else mbfg[bi2]
                        S.op("tensor", lambda e, wb=wb, k=k, bank=bank, tw=tw: e.matmul(
                            bank[:, 0:tw], wb[:, 128 * (k % 8):128 * (k % 8 + 1)],
                            ym_c[k][:, 0:tw], start=(k == 0), stop=(k == 15)))
                    t_op = S.fence("tensor", c_t)
                    wbf_mark[bi] = t_op
                    wbf_mark[bi2] = t_op
                    S.wait("scalar", c_t, t_op)
                    S.op("scalar", lambda e, hc=hc, tw=tw, bank=bank: e.activation(
                        out=spre[hc][:, 0:tw], in_=bank[:, 0:tw], func=AF.Identity,
                        bias=s_outb[:, hc:hc + 1]))
                    s_op = S.fence("scalar", c_s)
                    S.wait("vector", c_s, s_op)
                    opev[hc] = S.fence("vector", c_v)
                s_spre = c_s.v
                for q in range(tw // 128):
                    pan = (t0 - HALO) // 128 + q
                    par = 0
                    S.wait("tensor", c_s, s_spre)
                    if 0 in fin_v:
                        S.wait("tensor", c_v, fin_v[0])
                    for c in range(8):
                        S.op("tensor", lambda e, c=c, q=q: e.transpose(
                            psT[:, 128 * c:128 * (c + 1)], spre[c][:, 128 * q:128 * (q + 1)],
                            s_eyeb[:]))
                    t_tr = S.fence("tensor", c_t)
                    if pan == 0:
                        S.wait("sync", c_st, g_spilled)
                    S.wait("sync", c_s, fin_s.get(par, 0))
                    S.wait("sync", c_v, fin_v.get(par, 0))
                    S.dma("sync", c_gs, gtp2[par][:], d_gsp[128 * pan:128 * (pan + 1), :])
                    S.dma("sync", c_gs, xfin[par][:], d_xin[128 * (pan + 1):128 * (pan + 2), :])
                    mk_g = c_gs.v
                    S.wait("vector", c_t, t_tr)
                    S.wait("vector", c_st, fin_st.get(par, 0))
                    S.wait("vector", c_gs, mk_g)
                    S.op("vector", lambda e, par=par, pan=pan: e.tensor_scalar(
                        out=xnf[:], in0=xfin[par][:], scalar1=mr3[:, pan + 1:pan + 2, 0],
                        scalar2=mr3[:, pan + 1:pan + 2, 1], op0=OP.subtract, op1=OP.mult))
                    S.op("vector", lambda e: e.drain())
                    S.op("vector", lambda e: e.tensor_tensor(
                        out=sfin[:], in0=psT[:], in1=xnf[:], op=OP.add))
                    v_tr = S.fence("vector", c_v)
                    S.wait("scalar", c_gs, mk_g)
                    S.wait("scalar", c_v, v_tr)
                    S.op("scalar", lambda e, par=par: e.activation(
                        out=scr2[:], in_=gtp2[par][:], func=AF.Silu))
                    s_fin = S.fence("scalar", c_s)
                    fin_s[par] = s_fin
                    S.wait("vector", c_s, s_fin)
                    S.op("vector", lambda e, par=par: e.tensor_tensor(
                        out=ofin[par][:], in0=sfin[:], in1=scr2[:], op=OP.mult))
                    S.op("vector", lambda e: e.drain())
                    S.op("vector", lambda e, par=par: e.tensor_tensor(
                        out=ofin[par][:], in0=ofin[par][:], in1=xfin[par][:], op=OP.add))
                    fin_v[par] = S.fence("vector", c_v)
                    first_fin = False
                    S.wait("gpsimd", c_v, fin_v[par])
                    fin_st[par] = S.dma("gpsimd", c_st, d_out[128 * pan:128 * (pan + 1), :],
                                        ofin[par][:])
                    if debug:
                        S.dma("gpsimd", c_dbg, dbg["spre"][128 * pan:128 * (pan + 1), :], sfin[:])
                        dbg_mark = S.fence("gpsimd", c_g)
                        S.wait("vector", c_g, dbg_mark)

            S.wait("gpsimd", c_st, c_st.v)

        block = ctx.enter_context(nc.Block())

        @block.sync
        def _(sync):
            for eng, fn in S.items:
                if eng == "sync":
                    fn(sync)

        @block.vector
        def _(vector):
            for eng, fn in S.items:
                if eng == "vector":
                    fn(vector)

        @block.scalar
        def _(scalar):
            for eng, fn in S.items:
                if eng == "scalar":
                    fn(scalar)

        @block.tensor
        def _(tensor):
            for eng, fn in S.items:
                if eng == "tensor":
                    fn(tensor)

        @block.gpsimd
        def _(gpsimd):
            for eng, fn in S.items:
                if eng == "gpsimd":
                    fn(gpsimd)

    return nc


def _prep_inputs(inputs):
    x = np.ascontiguousarray(np.asarray(inputs["L0_attn_hidden"])[0])
    asc = np.ascontiguousarray

    def vec2(v, cols):
        return asc(np.asarray(v).reshape(cols, 128).T)

    import ml_dtypes
    common = dict(
        winT=asc(np.asarray(inputs["L0_attn_in_w"]).T),
        inb=vec2(inputs["L0_attn_in_b"], 32),
        convw=asc(np.asarray(inputs["L0_attn_conv_w"])[:, 0, :].reshape(16, 128, 4)),
        convb=vec2(inputs["L0_attn_conv_b"], 16),
        xpT=asc(np.asarray(inputs["L0_attn_xp_w"]).T),
        dtpT=asc(np.asarray(inputs["L0_attn_dtp_w"]).T),
        dtpb=vec2(inputs["L0_attn_dtp_b"], 16),
        alog=asc(np.asarray(inputs["L0_attn_A_log"])),
        dvec=vec2(inputs["L0_attn_D"], 16),
        outT=asc(np.asarray(inputs["L0_attn_out_w"]).T),
        outb=vec2(inputs["L0_attn_out_b"], 8),
        lnw=vec2(inputs["L0_attn_ln_w"], 8),
        lnb=vec2(inputs["L0_attn_ln_b"], 8),
        gam=vec2(inputs["L0_ffn_ln_gamma"], 8),
        bet=vec2(inputs["L0_ffn_ln_beta"], 8),
        rT=asc(np.asarray(inputs["L0_ffn_router_w"]).T),
        guT=asc(np.asarray(inputs["L0_ffn_gate_up_w"]).transpose(0, 2, 1)),
        dnT=asc(np.asarray(inputs["L0_ffn_down_w"]).transpose(0, 2, 1)),
        sguT=asc(np.asarray(inputs["L0_ffn_shared_gate_up_w"]).T),
        sdnT=asc(np.asarray(inputs["L0_ffn_shared_down_w"]).T),
        eyeb=np.eye(128, dtype=ml_dtypes.bfloat16),
        eyef=np.eye(128, dtype=np.float32),
        ones1=np.ones((1, 128), dtype=np.float32),
    )
    in_maps = []
    for k in range(NCORE):
        m = dict(common)
        if k == 0:
            halo = np.zeros((HALO, H), dtype=np.float32)
            m["umask"] = np.zeros((128, 1), dtype=np.float32)
        else:
            halo = x[T * k - HALO:T * k]
            m["umask"] = np.ones((128, 1), dtype=np.float32)
        m["xin"] = np.ascontiguousarray(np.concatenate([halo, x[T * k:T * (k + 1)]], axis=0))
        in_maps.append(m)
    return in_maps


def kernel(**inputs):
    in_maps = _prep_inputs(inputs)
    nc = build_nc(debug=DEBUG)
    res = run_bass_kernel_spmd(nc, in_maps, core_ids=list(range(NCORE)))
    out = np.concatenate([np.asarray(r["out"]) for r in res.results], axis=0)
    if DEBUG:
        kernel.last_results = res.results
    return out.reshape(1, NCORE * T, H).astype(np.float32)


# version 12
# speedup vs baseline: 1.1922x; 1.1922x over previous
import contextlib
import numpy as np
import concourse.bass as bass
import concourse.mybir as mybir
from concourse.bass_utils import run_bass_kernel_spmd

F32 = mybir.dt.float32
BF16 = mybir.dt.bfloat16
U8 = mybir.dt.uint8
AF = mybir.ActivationFunctionType
OP = mybir.AluOpType
AX = mybir.AxisListType

NCORE = 8
T = 2048
HALO = 128
TTOT = T + HALO
H = 1024
DI = 2048
NST = 16
NEXP = 8
DFF = 3072
SFF = 1024
NPAN = TTOT // 128
OTW = 512
OTS = [(0, 128)] + [(HALO + OTW * i, OTW) for i in range(T // OTW)]
FTS = [(HALO + 512 * i, 512) for i in range(4)]

DEBUG = False


class Ctr:
    __slots__ = ("sem", "v")

    def __init__(self, sem):
        self.sem = sem
        self.v = 0


class Sched:
    def __init__(self):
        self.items = []

    def op(self, eng, fn):
        self.items.append((eng, fn))

    def wait(self, eng, ctr, val=None):
        v = ctr.v if val is None else val
        if v <= 0:
            return
        sem = ctr.sem
        self.items.append((eng, lambda e: e.wait_ge(sem, v)))

    def dma(self, eng, ctr, out, in_):
        ctr.v += 16
        sem = ctr.sem
        self.items.append((eng, lambda e: e.dma_start(out=out, in_=in_).then_inc(sem, 16)))
        return ctr.v

    def fence(self, eng, ctr):
        ctr.v += 1
        sem = ctr.sem
        self.items.append((eng, lambda e: e.drain().then_inc(sem, 1)))
        return ctr.v


ENGS = ("sync", "vector", "scalar", "tensor", "gpsimd")


def build_nc(debug=False):
    nc = bass.Bass("TRN2", target_bir_lowering=False)

    d_xin = nc.dram_tensor("xin", [TTOT, H], F32, kind="ExternalInput").ap()
    d_umask = nc.dram_tensor("umask", [128, 1], F32, kind="ExternalInput").ap()
    d_winT = nc.dram_tensor("winT", [H, 4096], F32, kind="ExternalInput").ap()
    d_inb = nc.dram_tensor("inb", [128, 32], F32, kind="ExternalInput").ap()
    d_convw = nc.dram_tensor("convw", [16, 128, 4], F32, kind="ExternalInput").ap()
    d_convb = nc.dram_tensor("convb", [128, 16], F32, kind="ExternalInput").ap()
    d_xpT = nc.dram_tensor("xpT", [DI, 96], F32, kind="ExternalInput").ap()
    d_dtpT = nc.dram_tensor("dtpT", [64, DI], F32, kind="ExternalInput").ap()
    d_dtpb = nc.dram_tensor("dtpb", [128, 16], F32, kind="ExternalInput").ap()
    d_alog = nc.dram_tensor("alog", [DI, NST], F32, kind="ExternalInput").ap()
    d_dvec = nc.dram_tensor("dvec", [128, 16], F32, kind="ExternalInput").ap()
    d_outT = nc.dram_tensor("outT", [DI, H], F32, kind="ExternalInput").ap()
    d_outb = nc.dram_tensor("outb", [128, 8], F32, kind="ExternalInput").ap()
    d_lnw = nc.dram_tensor("lnw", [128, 8], F32, kind="ExternalInput").ap()
    d_lnb = nc.dram_tensor("lnb", [128, 8], F32, kind="ExternalInput").ap()
    d_gam = nc.dram_tensor("gam", [128, 8], F32, kind="ExternalInput").ap()
    d_bet = nc.dram_tensor("bet", [128, 8], F32, kind="ExternalInput").ap()
    d_rT = nc.dram_tensor("rT", [H, 8], F32, kind="ExternalInput").ap()
    d_guT = nc.dram_tensor("guT", [NEXP, H, 2 * DFF], F32, kind="ExternalInput").ap()
    d_dnT = nc.dram_tensor("dnT", [NEXP, DFF, H], F32, kind="ExternalInput").ap()
    d_sguT = nc.dram_tensor("sguT", [H, 2 * SFF], F32, kind="ExternalInput").ap()
    d_sdnT = nc.dram_tensor("sdnT", [SFF, H], F32, kind="ExternalInput").ap()
    d_eyeb = nc.dram_tensor("eyeb", [128, 128], BF16, kind="ExternalInput").ap()
    d_eyef = nc.dram_tensor("eyef", [128, 128], F32, kind="ExternalInput").ap()
    d_ones = nc.dram_tensor("ones1", [1, 128], F32, kind="ExternalInput").ap()
    d_out = nc.dram_tensor("out", [T, H], F32, kind="ExternalOutput").ap()
    d_gsp = nc.dram_tensor("gspill", [T, H], BF16, kind="Internal").ap()
    b_guT = nc.dram_tensor("b_guT", [NEXP, H, 2 * DFF], BF16, kind="Internal").ap()
    b_dnT = nc.dram_tensor("b_dnT", [NEXP, DFF, H], BF16, kind="Internal").ap()
    b_sguT = nc.dram_tensor("b_sguT", [H, 2 * SFF], BF16, kind="Internal").ap()
    b_sdnT = nc.dram_tensor("b_sdnT", [SFF, H], BF16, kind="Internal").ap()
    b_winT = nc.dram_tensor("b_winT", [H, 4096], BF16, kind="Internal").ap()
    b_outT = nc.dram_tensor("b_outT", [DI, H], BF16, kind="Internal").ap()
    dbg = {}
    if debug:
        for nm, shp, dt in [("stats", [128, 8 * NPAN], F32), ("xn", [TTOT, H], F32), ("z", [H, TTOT], BF16),
                            ("g", [T, H], BF16), ("comb", [T, 8], F32),
                            ("xs", [DI, TTOT], BF16), ("dt", [DI, TTOT], BF16),
                            ("ym", [DI, TTOT], BF16), ("spre", [T, H], BF16)]:
            dbg[nm] = nc.dram_tensor("dbg_" + nm, shp, dt, kind="ExternalOutput").ap()

    S = Sched()

    with contextlib.ExitStack() as ctx:
        def mk_alloc(c):
            def sb(name, shape, dt):
                return c.enter_context(nc.sbuf_tensor(name, shape, dt))
            return sb

        sb = mk_alloc(ctx)

        def psum(name, shape, dt):
            return ctx.enter_context(nc.psum_tensor(name, shape, dt))

        def sem(name):
            return Ctr(ctx.enter_context(nc.semaphore(name)))

        zT = [sb(f"zT{c}", [128, TTOT], BF16) for c in range(8)]
        s_lnw = sb("s_lnw", [128, 8], F32)
        s_lnb = sb("s_lnb", [128, 8], F32)
        s_gam = sb("s_gam", [128, 8], F32)
        s_bet = sb("s_bet", [128, 8], F32)
        s_inb = sb("s_inb", [128, 32], F32)
        s_convb = sb("s_convb", [128, 16], F32)
        s_dtpb = sb("s_dtpb", [128, 16], F32)
        s_dvec = sb("s_dvec", [128, 16], F32)
        s_outb = sb("s_outb", [128, 8], F32)
        s_convw = [sb(f"s_convw{c}", [128, 4], F32) for c in range(16)]
        s_negA = sb("s_negA", [128, 16 * NST], F32)
        s_eyeb = sb("s_eyeb", [128, 128], BF16)
        s_ones = sb("s_ones", [1, 128], F32)
        s_umask = sb("s_umask", [128, 1], F32)
        s_xpT = [sb(f"s_xpT{c}", [128, 96], BF16) for c in range(16)]
        s_dtpT = sb("s_dtpT", [64, DI], F32)
        s_rT = [sb(f"s_rT{c}", [128, 8], BF16) for c in range(8)]
        carry = sb("carry", [128, 16 * NST], F32)
        convh = sb("convh", [128, 3 * 16], F32)
        mrs = sb("mrs", [128, 4 * NPAN], F32)
        scr2 = sb("scr2", [128, H], F32)

        psA = psum("psA", [128, 512], F32)
        psB = psum("psB", [128, 512], F32)
        psC = psum("psC", [128, 512], F32)
        psD = psum("psD", [128, 512], F32)
        psE = psum("psE", [128, 512], F32)
        psR = psum("psR", [128, 512], F32)
        psT = psum("psT", [128, 1024], BF16)
        psTf = psum("psTf", [128, 128], F32)

        c_ld = sem("c_ld")
        c_row = sem("c_row")
        c_st = sem("c_st")
        c_gs = sem("c_gs")
        c_v = sem("c_v")
        c_s = sem("c_s")
        c_t = sem("c_t")
        c_g = sem("c_g")
        c_dbg = sem("c_dbg")
        CTRS = {"sync": c_gs, "vector": c_v, "scalar": c_s, "tensor": c_t, "gpsimd": c_g}

        def barrier():
            marks = {e: S.fence(e, CTRS[e]) for e in ENGS}
            for e in ENGS:
                for e2 in ENGS:
                    if e2 != e:
                        S.wait(e, CTRS[e2], marks[e2])

        stg_mark = {}
        wbf_mark = {}

        for dst, src in [(s_lnw, d_lnw), (s_lnb, d_lnb), (s_gam, d_gam), (s_bet, d_bet),
                         (s_inb, d_inb), (s_convb, d_convb), (s_dtpb, d_dtpb),
                         (s_dvec, d_dvec), (s_outb, d_outb), (s_eyeb, d_eyeb),
                         (s_ones, d_ones), (s_umask, d_umask),
                         (s_dtpT, d_dtpT)]:
            S.dma("sync", c_ld, dst[:], src)
        for c in range(16):
            S.dma("sync", c_ld, s_convw[c][:], d_convw[c])
        for c in range(16):
            S.dma("sync", c_ld, s_negA[:, 16 * c:16 * c + 16], d_alog[128 * c:128 * (c + 1), :])
        ld_pre = c_ld.v

        with contextlib.ExitStack() as ctx0:
            sb0 = mk_alloc(ctx0)
            xpan = [sb0(f"xpan{i}", [128, H], F32) for i in range(2)]
            zpan = sb0("zpan", [128, H], BF16)
            stats = sb0("stats", [128, 4 * NPAN], F32)
            scr1 = sb0("scr1", [128, H], F32)
            tiny = sb0("tiny", [128, 8 * NPAN], F32)
            wcast = [sb0(f"wcast{i}", [128, 128], F32) for i in range(2)]

            for c in range(16):
                if c >= 2:
                    S.wait("sync", c_g, stg_mark.get(("wc", c % 2), 0))
                S.dma("sync", c_ld, wcast[c % 2][:, 0:96], d_xpT[128 * c:128 * (c + 1), :])
                S.wait("gpsimd", c_ld)
                dst, srcv = s_xpT[c], wcast[c % 2]
                S.op("gpsimd", lambda e, d=dst, s=srcv: e.tensor_copy(out=d[:], in_=s[:, 0:96]))
                stg_mark[("wc", c % 2)] = S.fence("gpsimd", c_g)
            for c in range(8):
                S.wait("sync", c_g, stg_mark[("wc", c % 2)])
                S.dma("sync", c_ld, wcast[c % 2][:, 0:8], d_rT[128 * c:128 * (c + 1), :])
                S.wait("gpsimd", c_ld)
                dst, srcv = s_rT[c], wcast[c % 2]
                S.op("gpsimd", lambda e, d=dst, s=srcv: e.tensor_copy(out=d[:], in_=s[:, 0:8]))
                stg_mark[("wc", c % 2)] = S.fence("gpsimd", c_g)

            cstg = [sb0(f"cstg{i}", [128, 2048], F32) for i in range(4)]
            cbf = [sb0(f"cbf{i}", [128, 2048], BF16) for i in range(4)]
            c_wc = sem("c_wc")
            cast_jobs = []
            for e_ in range(NEXP):
                cast_jobs.append((f"gu{e_}", d_guT[e_], b_guT[e_], H, 2 * DFF))
                cast_jobs.append((f"dn{e_}", d_dnT[e_], b_dnT[e_], DFF, H))
            cast_jobs.append(("sgu", d_sguT, b_sguT, H, 2 * SFF))
            cast_jobs.append(("sdn", d_sdnT, b_sdnT, SFF, H))
            cast_jobs.append(("win", d_winT, b_winT, H, 4096))
            cast_jobs.append(("out", d_outT, b_outT, DI, H))
            cast_done = {}
            wi = 0
            wc_mark = {}
            wst_mark = {}
            for (jname, srcT, dstT, RR, CC) in cast_jobs:
                for r0 in range(0, RR, 128):
                    for c0 in range(0, CC, 2048):
                        cw = min(2048, CC - c0)
                        bi = wi % 4
                        eng = "gpsimd" if wi % 2 == 0 else "vector"
                        if bi in wc_mark:
                            S.wait("sync", c_g if wc_mark[bi][0] == "gpsimd" else c_v,
                                   wc_mark[bi][1])
                        S.dma("sync", c_ld, cstg[bi][:, 0:cw], srcT[r0:r0 + 128, c0:c0 + cw])
                        S.wait(eng, c_ld)
                        if bi in wst_mark:
                            S.wait(eng, c_st, wst_mark[bi])
                        S.op(eng, lambda e, bi=bi, cw=cw: e.tensor_copy(
                            out=cbf[bi][:, 0:cw], in_=cstg[bi][:, 0:cw]))
                        mk = S.fence(eng, c_g if eng == "gpsimd" else c_v)
                        wc_mark[bi] = (eng, mk)
                        S.wait("scalar", c_g if eng == "gpsimd" else c_v, mk)
                        wst_mark[bi] = S.dma("scalar", c_st, dstT[r0:r0 + 128, c0:c0 + cw],
                                             cbf[bi][:, 0:cw])
                        wi += 1
                cast_done[jname] = c_st.v
            w_cast_done = c_st.v

            S.wait("scalar", c_ld, ld_pre)
            S.op("scalar", lambda e: e.activation(out=s_negA[:], in_=s_negA[:], func=AF.Exp))
            sm = S.fence("scalar", c_s)
            S.wait("vector", c_s, sm)
            S.op("vector", lambda e: e.tensor_scalar(out=s_negA[:], in0=s_negA[:], scalar1=-1.0,
                                                     scalar2=None, op0=OP.mult))
            S.op("vector", lambda e: e.memset(carry[:], 0.0))
            S.op("vector", lambda e: e.memset(convh[:], 0.0))
            v_prol = S.fence("vector", c_v)

            p1m = []
            for p in range(NPAN):
                if p >= 2:
                    S.wait("sync", c_s, p1m[p - 2])
                S.dma("sync", c_ld, xpan[p % 2][:], d_xin[128 * p:128 * (p + 1), :])
                S.wait("scalar", c_ld)
                xb = xpan[p % 2]
                S.op("scalar", lambda e, xb=xb, p=p: e.activation(
                    out=scr1[:], in_=xb[:], func=AF.Identity, accum_out=stats[:, 4 * p:4 * p + 1]))
                S.op("scalar", lambda e, xb=xb, p=p: e.activation(
                    out=scr1[:], in_=xb[:], func=AF.Square, accum_out=stats[:, 4 * p + 1:4 * p + 2]))
                p1m.append(S.fence("scalar", c_s))

            st3 = stats[:].rearrange("p (a b) -> p a b", b=4)
            mr3 = mrs[:].rearrange("p (a b) -> p a b", b=4)
            ty3 = tiny[:].rearrange("p (a b) -> p a b", b=8)

            def stat_math(i_sum, i_sq, o_mu, o_rstd):
                S.op("vector", lambda e: e.tensor_scalar(
                    out=mr3[:, :, o_mu], in0=st3[:, :, i_sum], scalar1=1.0 / H, scalar2=None,
                    op0=OP.mult))
                S.op("vector", lambda e: e.tensor_scalar(
                    out=ty3[:, :, 0], in0=st3[:, :, i_sq], scalar1=1.0 / H, scalar2=None,
                    op0=OP.mult))
                S.op("vector", lambda e: e.drain())
                S.op("vector", lambda e: e.tensor_tensor(
                    out=ty3[:, :, 1], in0=mr3[:, :, o_mu], in1=mr3[:, :, o_mu], op=OP.mult))
                S.op("vector", lambda e: e.drain())
                S.op("vector", lambda e: e.tensor_tensor(
                    out=ty3[:, :, 2], in0=ty3[:, :, 0], in1=ty3[:, :, 1], op=OP.subtract))
                S.op("vector", lambda e: e.drain())
                S.op("vector", lambda e: e.tensor_scalar(
                    out=ty3[:, :, 2], in0=ty3[:, :, 2], scalar1=1e-5, scalar2=None, op0=OP.add))
                S.op("vector", lambda e: e.drain())
                vm = S.fence("vector", c_v)
                S.wait("scalar", c_v, vm)
                S.op("scalar", lambda e: e.activation(out=ty3[:, :, 3], in_=ty3[:, :, 2],
                                                      func=AF.Sqrt))
                sm2 = S.fence("scalar", c_s)
                S.wait("vector", c_s, sm2)
                S.op("vector", lambda e: e.reciprocal(out=mr3[:, :, o_rstd], in_=ty3[:, :, 3]))
                S.op("vector", lambda e: e.drain())
                return S.fence("vector", c_v)

            S.wait("vector", c_s, p1m[-1])
            S.wait("vector", c_v, v_prol)
            v_ln1 = stat_math(0, 1, 0, 1)

            p2v, p2s = [], []
            for p in range(NPAN):
                if p < 2:
                    S.wait("sync", c_s, p1m[-1])
                if p >= 2:
                    S.wait("sync", c_v, p2v[p - 2])
                S.dma("sync", c_ld, xpan[p % 2][:], d_xin[128 * p:128 * (p + 1), :])
                S.wait("vector", c_ld)
                if p == 0:
                    S.wait("vector", c_v, v_ln1)
                else:
                    S.wait("vector", c_s, p2s[p - 1])
                xb = xpan[p % 2]
                S.op("vector", lambda e, xb=xb, p=p: e.tensor_scalar(
                    out=scr2[:], in0=xb[:], scalar1=mr3[:, p:p + 1, 0], scalar2=mr3[:, p:p + 1, 1],
                    op0=OP.subtract, op1=OP.mult))
                vm = S.fence("vector", c_v)
                p2v.append(vm)
                S.wait("scalar", c_v, vm)
                S.op("scalar", lambda e, p=p: e.activation(
                    out=scr1[:], in_=scr2[:], func=AF.Identity,
                    accum_out=stats[:, 4 * p + 2:4 * p + 3]))
                S.op("scalar", lambda e, p=p: e.activation(
                    out=scr1[:], in_=scr2[:], func=AF.Square,
                    accum_out=stats[:, 4 * p + 3:4 * p + 4]))
                p2s.append(S.fence("scalar", c_s))
            S.wait("vector", c_s, p2s[-1])
            v_ln2 = stat_math(2, 3, 2, 3)

            zv = []
            for p in range(NPAN):
                if p < 2:
                    S.wait("sync", c_s, p2s[-1])
                    S.wait("sync", c_v, p2v[-1])
                if p >= 2:
                    S.wait("sync", c_v, zv[p - 2])
                S.dma("sync", c_ld, xpan[p % 2][:], d_xin[128 * p:128 * (p + 1), :])
                S.wait("vector", c_ld)
                if p == 0:
                    S.wait("vector", c_v, v_ln2)
                xb = xpan[p % 2]
                S.op("vector", lambda e, xb=xb, p=p: e.tensor_scalar(
                    out=scr2[:], in0=xb[:], scalar1=mr3[:, p:p + 1, 0], scalar2=mr3[:, p:p + 1, 1],
                    op0=OP.subtract, op1=OP.mult))
                S.op("vector", lambda e: e.drain())
                S.op("vector", lambda e, p=p: e.tensor_scalar(
                    out=zpan[:], in0=scr2[:], scalar1=mr3[:, p:p + 1, 2], scalar2=mr3[:, p:p + 1, 3],
                    op0=OP.subtract, op1=OP.mult))
                vm = S.fence("vector", c_v)
                zv.append(vm)
                S.wait("tensor", c_v, vm)
                for c in range(8):
                    S.op("tensor", lambda e, c=c: e.transpose(
                        psT[:, 128 * c:128 * (c + 1)], zpan[:, 128 * c:128 * (c + 1)], s_eyeb[:]))
                tm = S.fence("tensor", c_t)
                S.wait("vector", c_t, tm)
                for c in range(8):
                    S.op("vector", lambda e, c=c, p=p: e.tensor_copy(
                        out=zT[c][:, 128 * p:128 * (p + 1)], in_=psT[:, 128 * c:128 * (c + 1)]))
                vm2 = S.fence("vector", c_v)
                S.wait("tensor", c_v, vm2)
            if debug:
                vz = S.fence("vector", c_v)
                S.wait("gpsimd", c_v, vz)
                for c in range(8):
                    S.dma("gpsimd", c_dbg, dbg["z"][128 * c:128 * (c + 1), :], zT[c][:])
                S.dma("gpsimd", c_dbg, dbg["stats"][:, 0:4 * NPAN], stats[:])
                S.dma("gpsimd", c_dbg, dbg["stats"][:, 4 * NPAN:8 * NPAN], mrs[:])
                S.op("gpsimd", lambda e: e.drain())
            barrier()

        with contextlib.ExitStack() as ctx1:
            sb1 = mk_alloc(ctx1)
            normT = [sb1(f"normT{c}", [128, 512], BF16) for c in range(8)]
            hff = [sb1(f"hff{j}", [128, 512], BF16) for j in range(24)]
            gacc = [sb1(f"gacc{c}", [128, T], BF16) for c in range(8)]
            wbfg = [sb1(f"wbfg{i}", [128, 1024], BF16) for i in range(8)]
            wdnb = [sb1(f"wdnb{i}", [128, 3072], BF16) for i in range(2)]
            cbc = [sb1(f"cbc{i}", [128, 512], BF16) for i in range(2)]
            crsc = [sb1(f"crsc{i}", [1, 512], F32) for i in range(2)]
            sgb = [sb1(f"sgb{i}", [128, 512], BF16) for i in range(2)]
            upe = [sb1(f"upe{i}", [128, 512], BF16) for i in range(2)]
            logits_sb = sb1("logits_sb", [8, 512], F32)
            rsc = sb1("rsc", [128, 64], F32)
            rmsk = sb1("rmsk", [128, 16], U8)
            combs = [sb1(f"comb{p}", [128, 8], F32) for p in range(16)]
            combrow = sb1("combrow", [8, T], F32)
            s_eyef = sb1("s_eyef", [128, 128], F32)
            S.dma("sync", c_ld, s_eyef[:], d_eyef)
            S.wait("tensor", c_ld)
            gtp = [sb1(f"gtp{i}", [128, H], BF16) for i in range(2)]

            for ti, (t0, tw) in enumerate(FTS):
                for c in range(8):
                    S.op("vector", lambda e, c=c, t0=t0, tw=tw: e.tensor_scalar(
                        out=normT[c][:, 0:tw], in0=zT[c][:, t0:t0 + tw],
                        scalar1=s_gam[:, c:c + 1], scalar2=s_bet[:, c:c + 1],
                        op0=OP.mult, op1=OP.add))
                vm = S.fence("vector", c_v)
                S.wait("tensor", c_v, vm)
                for c in range(8):
                    S.op("tensor", lambda e, c=c, tw=tw: e.matmul(
                        psR[0:8, 0:tw], s_rT[c][:], normT[c][:, 0:tw],
                        start=(c == 0), stop=(c == 7)))
                tm = S.fence("tensor", c_t)
                S.wait("vector", c_t, tm)
                S.op("vector", lambda e, tw=tw: e.tensor_copy(out=logits_sb[:, 0:tw],
                                                              in_=psR[0:8, 0:tw]))
                v_lg = S.fence("vector", c_v)
                for q in range(tw // 128):
                    pan = ti * 4 + q
                    S.wait("tensor", c_v, v_lg)
                    S.op("tensor", lambda e, q=q: e.transpose(
                        psTf[:, 0:8], logits_sb[:, 128 * q:128 * (q + 1)], s_eyef[0:8, 0:8]))
                    tm2 = S.fence("tensor", c_t)
                    S.wait("vector", c_t, tm2)
                    S.op("vector", lambda e: e.tensor_copy(out=rsc[:, 0:8], in_=psTf[:, 0:8]))
                    vq = S.fence("vector", c_v)
                    S.wait("tensor", c_v, vq)
                    S.op("vector", lambda e: e.tensor_reduce(out=rsc[:, 8:9], in_=rsc[:, 0:8],
                                                             axis=AX.X, op=OP.max, negate=True))
                    S.op("vector", lambda e: e.drain())
                    vq2 = S.fence("vector", c_v)
                    S.wait("scalar", c_v, vq2)
                    S.op("scalar", lambda e: e.activation(out=rsc[:, 16:24], in_=rsc[:, 0:8],
                                                          func=AF.Exp, bias=rsc[:, 8:9]))
                    sq = S.fence("scalar", c_s)
                    S.wait("vector", c_s, sq)
                    S.op("vector", lambda e: e.tensor_reduce(out=rsc[:, 9:10], in_=rsc[:, 16:24],
                                                             axis=AX.X, op=OP.add))
                    S.op("vector", lambda e: e.drain())
                    S.op("vector", lambda e: e.reciprocal(out=rsc[:, 10:11], in_=rsc[:, 9:10]))
                    S.op("vector", lambda e: e.drain())
                    S.op("vector", lambda e: e.tensor_scalar(
                        out=rsc[:, 24:32], in0=rsc[:, 16:24], scalar1=rsc[:, 10:11], scalar2=None,
                        op0=OP.mult))
                    S.op("vector", lambda e: e.drain())
                    S.op("vector", lambda e: e.tensor_reduce(out=rsc[:, 11:12], in_=rsc[:, 24:32],
                                                             axis=AX.X, op=OP.max))
                    S.op("vector", lambda e: e.drain())
                    S.op("vector", lambda e: e.tensor_scalar(
                        out=rmsk[:, 0:8], in0=rsc[:, 24:32], scalar1=rsc[:, 11:12], scalar2=None,
                        op0=OP.is_ge))
                    S.op("vector", lambda e: e.drain())
                    S.op("vector", lambda e: e.tensor_copy(out=rsc[:, 32:40], in_=rmsk[:, 0:8]))
                    S.op("vector", lambda e: e.drain())
                    S.op("vector", lambda e: e.scalar_tensor_tensor(
                        out=rsc[:, 40:48], in0=rsc[:, 32:40], scalar=-1e30, in1=rsc[:, 24:32],
                        op0=OP.mult, op1=OP.add))
                    S.op("vector", lambda e: e.drain())
                    S.op("vector", lambda e: e.tensor_reduce(out=rsc[:, 12:13], in_=rsc[:, 40:48],
                                                             axis=AX.X, op=OP.max))
                    S.op("vector", lambda e: e.drain())
                    S.op("vector", lambda e: e.tensor_scalar(
                        out=rmsk[:, 8:16], in0=rsc[:, 40:48], scalar1=rsc[:, 12:13], scalar2=None,
                        op0=OP.is_ge))
                    S.op("vector", lambda e: e.drain())
                    S.op("vector", lambda e: e.tensor_copy(out=rsc[:, 48:56], in_=rmsk[:, 8:16]))
                    S.op("vector", lambda e: e.drain())
                    vq3 = S.fence("vector", c_v)
                    S.wait("scalar", c_v, vq3)
                    S.op("scalar", lambda e: e.activation(out=rsc[:, 13:14], in_=rsc[:, 11:12],
                                                          func=AF.Exp))
                    S.op("scalar", lambda e: e.activation(out=rsc[:, 14:15], in_=rsc[:, 12:13],
                                                          func=AF.Exp))
                    sq2 = S.fence("scalar", c_s)
                    S.wait("vector", c_s, sq2)
                    S.op("vector", lambda e: e.tensor_tensor(out=rsc[:, 15:16], in0=rsc[:, 13:14],
                                                             in1=rsc[:, 14:15], op=OP.add))
                    S.op("vector", lambda e: e.drain())
                    S.op("vector", lambda e: e.reciprocal(out=rsc[:, 56:57], in_=rsc[:, 15:16]))
                    S.op("vector", lambda e: e.drain())
                    S.op("vector", lambda e: e.tensor_scalar(
                        out=rsc[:, 13:14], in0=rsc[:, 13:14], scalar1=rsc[:, 56:57], scalar2=None,
                        op0=OP.mult))
                    S.op("vector", lambda e: e.tensor_scalar(
                        out=rsc[:, 14:15], in0=rsc[:, 14:15], scalar1=rsc[:, 56:57], scalar2=None,
                        op0=OP.mult))
                    S.op("vector", lambda e: e.drain())
                    S.op("vector", lambda e, pan=pan: e.tensor_scalar(
                        out=combs[pan][:], in0=rsc[:, 32:40], scalar1=rsc[:, 13:14], scalar2=None,
                        op0=OP.mult))
                    S.op("vector", lambda e: e.drain())
                    S.op("vector", lambda e, pan=pan: e.scalar_tensor_tensor(
                        out=combs[pan][:], in0=rsc[:, 48:56], scalar=rsc[:, 14:15],
                        in1=combs[pan][:], op0=OP.mult, op1=OP.add))
                    S.op("vector", lambda e: e.drain())
            v_comb = S.fence("vector", c_v)
            if debug:
                S.wait("gpsimd", c_v, v_comb)
                for pan in range(16):
                    S.dma("gpsimd", c_dbg, dbg["comb"][128 * pan:128 * (pan + 1), :], combs[pan][:])
                S.op("gpsimd", lambda e: e.drain())
            for pan in range(16):
                S.wait("tensor", c_v, v_comb)
                S.op("tensor", lambda e, pan=pan: e.transpose(psTf[0:8, :], combs[pan][:],
                                                              s_eyef[:]))
                tm = S.fence("tensor", c_t)
                S.wait("vector", c_t, tm)
                S.op("vector", lambda e, pan=pan: e.tensor_copy(
                    out=combrow[:, 128 * pan:128 * (pan + 1)], in_=psTf[0:8, :]))
                vm = S.fence("vector", c_v)
                S.wait("tensor", c_v, vm)
            v_combrow = c_v.v

            def gu_block(e_, j, half):
                f0 = j * 128 + half * DFF
                return b_guT[e_].rearrange("(k p) f -> p k f", p=128)[:, :, f0:f0 + 128]

            def sgu_block(j, half):
                f0 = j * 128 + half * SFF
                return b_sguT.rearrange("(k p) f -> p k f", p=128)[:, :, f0:f0 + 128]

            last_pair_mm = 0
            last_dn_mm = 0
            cbc_mark = {}
            crsc_mark = {}
            dstg_mark = {}
            dwbf_mark = {}
            for e_ in range(NEXP + 1):
                shared = (e_ == NEXP)
                npair = 8 if shared else 24
                for ti, (t0, tw) in enumerate(FTS):
                    S.wait("vector", c_t, last_pair_mm)
                    for c in range(8):
                        S.op("vector", lambda e, c=c, t0=t0, tw=tw: e.tensor_scalar(
                            out=normT[c][:, 0:tw], in0=zT[c][:, t0:t0 + tw],
                            scalar1=s_gam[:, c:c + 1], scalar2=s_bet[:, c:c + 1],
                            op0=OP.mult, op1=OP.add))
                    v_norm = S.fence("vector", c_v)
                    if not shared:
                        S.wait("scalar", c_v, v_combrow)
                        S.wait("scalar", c_t, crsc_mark.get(ti % 2, 0))
                        S.dma("scalar", c_row, crsc[ti % 2][:, 0:tw],
                              combrow[e_:e_ + 1, t0 - HALO:t0 - HALO + tw])
                        mk_cr = c_row.v
                        S.wait("tensor", c_row, mk_cr)
                        S.wait("tensor", c_v, cbc_mark.get(ti % 2, v_combrow))
                        S.op("tensor", lambda e, ti=ti, tw=tw: e.matmul(
                            psR[:, 0:tw], s_ones[:], crsc[ti % 2][:, 0:tw],
                            start=True, stop=True))
                        tmc = S.fence("tensor", c_t)
                        crsc_mark[ti % 2] = tmc
                        S.wait("vector", c_t, tmc)
                        S.op("vector", lambda e, ti=ti, tw=tw: e.tensor_copy(
                            out=cbc[ti % 2][:, 0:tw], in_=psR[:, 0:tw]))
                        vmc = S.fence("vector", c_v)
                        cbc_mark[ti % 2] = vmc
                        S.wait("tensor", c_v, vmc)
                    pair_v = {}
                    pair_s = {}
                    if ti == 0:
                        S.wait("sync", c_st,
                               cast_done["sgu"] if shared else cast_done[f"gu{e_}"])
                        S.wait("sync", c_st,
                               cast_done["sdn"] if shared else cast_done[f"dn{e_}"])
                    for j in range(npair):
                        for half in (0, 1):
                            bi = (2 * j + half) % 8
                            S.wait("sync", c_t, wbf_mark.get(bi, 0))
                            src = sgu_block(j, half) if shared else gu_block(e_, j, half)
                            S.dma("sync", c_ld, wbfg[bi][:].rearrange("p (k f) -> p k f", k=8), src)
                            stg_mark[bi] = c_ld.v
                        S.wait("tensor", c_ld, stg_mark[(2 * j + 1) % 8])
                        if j == 0:
                            S.wait("tensor", c_v, v_norm)
                        if j >= 2:
                            S.wait("tensor", c_v, pair_v[j - 2])
                            S.wait("tensor", c_s, pair_s[j - 2])
                        bkg, bku = (psA, psB) if j % 2 == 0 else (psC, psD)
                        for half, bank in ((0, bkg), (1, bku)):
                            bi = (2 * j + half) % 8
                            for k in range(8):
                                S.op("tensor", lambda e, bi=bi, k=k, bank=bank, tw=tw: e.matmul(
                                    bank[:, 0:tw], wbfg[bi][:, 128 * k:128 * (k + 1)],
                                    normT[k][:, 0:tw], start=(k == 0), stop=(k == 7)))
                        tmm = S.fence("tensor", c_t)
                        wbf_mark[(2 * j) % 8] = tmm
                        wbf_mark[(2 * j + 1) % 8] = tmm
                        last_pair_mm = tmm
                        S.wait("scalar", c_t, tmm)
                        if j >= 2:
                            S.wait("scalar", c_v, pair_v[j - 2])
                        S.op("scalar", lambda e, j=j, tw=tw, bkg=bkg: e.activation(
                            out=sgb[j % 2][:, 0:tw], in_=bkg[:, 0:tw], func=AF.Silu))
                        smk = S.fence("scalar", c_s)
                        pair_s[j] = smk
                        S.wait("vector", c_t, tmm)
                        S.op("vector", lambda e, j=j, tw=tw, bku=bku: e.tensor_copy(
                            out=upe[j % 2][:, 0:tw], in_=bku[:, 0:tw]))
                        S.wait("vector", c_s, smk)
                        S.wait("vector", c_t, last_dn_mm)
                        S.op("vector", lambda e, j=j, tw=tw: e.tensor_tensor(
                            out=hff[j][:, 0:tw], in0=sgb[j % 2][:, 0:tw], in1=upe[j % 2][:, 0:tw],
                            op=OP.mult))
                        if not shared:
                            S.op("vector", lambda e, j=j, ti=ti, tw=tw: e.tensor_tensor(
                                out=hff[j][:, 0:tw], in0=hff[j][:, 0:tw], in1=cbc[ti % 2][:, 0:tw],
                                op=OP.mult))
                        pair_v[j] = S.fence("vector", c_v)
                    v_hff = pair_v[npair - 1]
                    dn_ev = {}
                    for c in range(8):
                        di = c % 2
                        S.wait("sync", c_t, dwbf_mark.get(di, 0))
                        src = (b_sdnT if shared else b_dnT[e_]).rearrange(
                            "(k p) f -> p k f", p=128)[:, 0:npair, 128 * c:128 * (c + 1)]
                        S.dma("sync", c_ld,
                              wdnb[di][:, 0:128 * npair].rearrange("p (k f) -> p k f", k=npair),
                              src)
                        dstg_mark[di] = c_ld.v
                        S.wait("tensor", c_ld, dstg_mark[di])
                        if c == 0:
                            S.wait("tensor", c_v, v_hff)
                        if c >= 2:
                            S.wait("tensor", c_v, dn_ev[c - 2])
                        bank = psD if c % 2 == 0 else psE
                        for k in range(npair):
                            S.op("tensor", lambda e, di=di, k=k, bank=bank, tw=tw: e.matmul(
                                bank[:, 0:tw], wdnb[di][:, 128 * k:128 * (k + 1)], hff[k][:, 0:tw],
                                start=(k == 0), stop=(k == npair - 1)))
                        tdn = S.fence("tensor", c_t)
                        dwbf_mark[di] = tdn
                        last_dn_mm = tdn
                        S.wait("vector", c_t, tdn)
                        if e_ == 0:
                            S.op("vector", lambda e, c=c, t0=t0, tw=tw, bank=bank: e.tensor_copy(
                                out=gacc[c][:, t0 - HALO:t0 - HALO + tw], in_=bank[:, 0:tw]))
                        else:
                            S.op("vector", lambda e, c=c, t0=t0, tw=tw, bank=bank: e.tensor_tensor(
                                out=gacc[c][:, t0 - HALO:t0 - HALO + tw], in0=bank[:, 0:tw],
                                in1=gacc[c][:, t0 - HALO:t0 - HALO + tw], op=OP.add))
                        dn_ev[c] = S.fence("vector", c_v)
            v_ffn = c_v.v

            gst = []
            for pan in range(16):
                S.wait("tensor", c_v, v_ffn)
                if pan >= 2:
                    S.wait("vector", c_st, gst[pan - 2])
                for c in range(8):
                    S.op("tensor", lambda e, c=c, pan=pan: e.transpose(
                        psT[:, 128 * c:128 * (c + 1)], gacc[c][:, 128 * pan:128 * (pan + 1)],
                        s_eyeb[:]))
                tm = S.fence("tensor", c_t)
                S.wait("vector", c_t, tm)
                S.op("vector", lambda e, pan=pan: e.tensor_copy(out=gtp[pan % 2][:], in_=psT[:]))
                vm = S.fence("vector", c_v)
                S.wait("tensor", c_v, vm)
                S.wait("gpsimd", c_v, vm)
                gst.append(S.dma("gpsimd", c_st, d_gsp[128 * pan:128 * (pan + 1), :],
                                 gtp[pan % 2][:]))
            if debug:
                S.wait("gpsimd", c_st, c_st.v)
                for pan in range(16):
                    S.dma("gpsimd", c_dbg, dbg["g"][128 * pan:128 * (pan + 1), :],
                          d_gsp[128 * pan:128 * (pan + 1), :])
                S.op("gpsimd", lambda e: e.drain())
            barrier()
        g_spilled = c_st.v
        stg_mark = {}
        wbf_mark = {}

        with contextlib.ExitStack() as ctx2:
            sb2 = mk_alloc(ctx2)
            xs_c = [sb2(f"xs{c}", [128, OTW], BF16) for c in range(16)]
            gt_c = [sb2(f"gt{c}", [128, OTW], BF16) for c in range(16)]
            ym_c = [sb2(f"ym{c}", [128, OTW], BF16) for c in range(16)]
            dtb = [sb2(f"dtb{c}", [128, OTW], BF16) for c in range(16)]
            bbc = [sb2(f"bbc{n}", [128, OTW], BF16) for n in range(16)]
            ccb = [sb2(f"ccb{n}", [128, OTW], BF16) for n in range(16)]
            params = sb2("params", [128, OTW], F32)
            brow_t = [sb2("brow0", [1, OTW], F32)]
            crow_t = [sb2("crow0", [1, OTW], F32)]
            u_t = [sb2(f"u{i}", [128, OTW], BF16) for i in range(2)]
            dec_t = [sb2(f"dec{i}", [128, OTW], BF16) for i in range(4)]
            uu_t = [sb2(f"uu{i}", [128, OTW], BF16) for i in range(4)]
            ss_t = [sb2(f"ss{i}", [128, OTW], F32) for i in range(4)]
            sm_t = [sb2(f"sm{i}", [128, OTW], BF16) for i in range(4)]
            cscr = [sb2(f"cscr{i}", [128, OTW], F32) for i in range(2)]
            xm_c = [sb2(f"xm{i}", [128, OTW + 3], F32) for i in range(2)]
            mbfg = [sb2(f"mbfg{i}", [128, 1024], BF16) for i in range(4)]
            spre = bbc[8:16]
            hnT = bbc[0:8]
            gtp2 = [sb2("gtp2_0", [128, H], BF16)]
            xfin = [sb2("xfin0", [128, H], F32)]
            xnf = sb2("xnf", [128, H], BF16)
            sfin = sb2("sfin", [128, H], BF16)
            ofin = [sb2("ofin0", [128, H], F32)]

            def win_block(f0):
                return b_winT.rearrange("(k p) f -> p k f", p=128)[:, :, f0:f0 + 128]

            def out_block(h0, klo):
                return b_outT.rearrange("(k p) f -> p k f", p=128)[:, klo:klo + 8,
                                                                   128 * h0:128 * (h0 + 1)]

            last_ip_mm = 0
            last_grp_g = 0
            dbg_mark = 0
            fin_s = {}
            fin_v = {}
            fin_st = {}
            gld_v = {}
            first_fin = True
            for oi, (t0, tw) in enumerate(OTS):
                is_halo = (oi == 0)
                S.wait("vector", c_t, last_ip_mm)
                if debug:
                    S.wait("vector", c_g, dbg_mark)
                    S.wait("scalar", c_g, dbg_mark)
                for c in range(8):
                    S.op("vector", lambda e, c=c, t0=t0, tw=tw: e.tensor_scalar(
                        out=hnT[c][:, 0:tw], in0=zT[c][:, t0:t0 + tw],
                        scalar1=s_lnw[:, c:c + 1], scalar2=s_lnb[:, c:c + 1],
                        op0=OP.mult, op1=OP.add))
                v_hn = S.fence("vector", c_v)
                ipev = {}
                for f in range(32):
                    bi = f % 2
                    if oi == 0 and f == 0:
                        S.wait("sync", c_st, cast_done["win"])
                        S.wait("sync", c_st, cast_done["out"])
                    S.wait("sync", c_t, wbf_mark.get(bi, 0))
                    S.dma("sync", c_ld, mbfg[bi][:].rearrange("p (k f) -> p k f", k=8),
                          win_block(128 * f))
                    stg_mark[bi] = c_ld.v
                    S.wait("tensor", c_ld, stg_mark[bi])
                    if f == 0:
                        S.wait("tensor", c_v, v_hn)
                    if f >= 2:
                        S.wait("tensor", c_v, ipev[f - 2])
                    bank = psA if f % 2 == 0 else psB
                    for k in range(8):
                        S.op("tensor", lambda e, bi=bi, k=k, bank=bank, tw=tw: e.matmul(
                            bank[:, 0:tw], mbfg[bi][:, 128 * k:128 * (k + 1)], hnT[k][:, 0:tw],
                            start=(k == 0), stop=(k == 7)))
                    tmm = S.fence("tensor", c_t)
                    wbf_mark[bi] = tmm
                    last_ip_mm = tmm
                    S.wait("scalar", c_t, tmm)
                    if f < 16:
                        ci = f
                        S.op("scalar", lambda e, ci=ci, tw=tw, bank=bank: e.activation(
                            out=xm_c[ci % 2][:, 3:3 + tw], in_=bank[:, 0:tw], func=AF.Identity,
                            bias=s_inb[:, ci:ci + 1]))
                        smk = S.fence("scalar", c_s)
                        S.wait("vector", c_s, smk)
                        S.op("vector", lambda e, ci=ci: e.tensor_copy(
                            out=xm_c[ci % 2][:, 0:3], in_=convh[:, 3 * ci:3 * ci + 3]))
                        if is_halo:
                            S.op("vector", lambda e, ci=ci, tw=tw: e.tensor_scalar(
                                out=xm_c[ci % 2][:, 3:3 + tw], in0=xm_c[ci % 2][:, 3:3 + tw],
                                scalar1=s_umask[:], scalar2=None, op0=OP.mult))
                        S.op("vector", lambda e: e.drain())
                        S.op("vector", lambda e, ci=ci, tw=tw: e.tensor_copy(
                            out=convh[:, 3 * ci:3 * ci + 3], in_=xm_c[ci % 2][:, tw:tw + 3]))
                        if ci >= 2:
                            S.wait("vector", c_s, conv_s[ci - 2])
                        S.op("vector", lambda e, ci=ci, tw=tw: e.tensor_scalar(
                            out=cscr[ci % 2][:, 0:tw], in0=xm_c[ci % 2][:, 0:tw],
                            scalar1=s_convw[ci][:, 0:1], scalar2=None, op0=OP.mult))
                        for kk in range(1, 4):
                            S.op("vector", lambda e, ci=ci, tw=tw, kk=kk: e.scalar_tensor_tensor(
                                out=cscr[ci % 2][:, 0:tw], in0=xm_c[ci % 2][:, kk:kk + tw],
                                scalar=s_convw[ci][:, kk:kk + 1], in1=cscr[ci % 2][:, 0:tw],
                                op0=OP.mult, op1=OP.add))
                        vmk = S.fence("vector", c_v)
                        S.wait("scalar", c_v, vmk)
                        S.op("scalar", lambda e, ci=ci, tw=tw: e.activation(
                            out=xs_c[ci][:, 0:tw], in_=cscr[ci % 2][:, 0:tw], func=AF.Silu,
                            bias=s_convb[:, ci:ci + 1]))
                        if ci == 0:
                            conv_s = {}
                        conv_s[ci] = S.fence("scalar", c_s)
                        ipev[f] = S.fence("vector", c_v)
                    else:
                        ci = f - 16
                        S.op("scalar", lambda e, ci=ci, tw=tw, bank=bank: e.activation(
                            out=gt_c[ci][:, 0:tw], in_=bank[:, 0:tw], func=AF.Identity,
                            bias=s_inb[:, 16 + ci:17 + ci]))
                        smk = S.fence("scalar", c_s)
                        S.wait("vector", c_s, smk)
                        ipev[f] = S.fence("vector", c_v)
                v_xs = c_v.v
                s_xs = c_s.v
                if debug and not is_halo:
                    S.wait("gpsimd", c_s, s_xs)
                    S.wait("gpsimd", c_v, v_xs)
                    for c in range(16):
                        S.dma("gpsimd", c_dbg, dbg["xs"][128 * c:128 * (c + 1), t0:t0 + tw],
                              xs_c[c][:, 0:tw])
                    dbg_mark = S.fence("gpsimd", c_g)
                if not is_halo:
                    for ci in range(16):
                        S.op("scalar", lambda e, ci=ci, tw=tw: e.activation(
                            out=gt_c[ci][:, 0:tw], in_=gt_c[ci][:, 0:tw], func=AF.Silu))
                s_sgy = S.fence("scalar", c_s)
                S.wait("tensor", c_s, s_xs)
                S.wait("tensor", c_v, v_xs)
                for k in range(16):
                    S.op("tensor", lambda e, k=k, tw=tw: e.matmul(
                        psC[0:96, 0:tw], s_xpT[k][:], xs_c[k][:, 0:tw],
                        start=(k == 0), stop=(k == 15)))
                t_par = S.fence("tensor", c_t)
                S.wait("scalar", c_t, t_par)
                S.op("scalar", lambda e, tw=tw: e.activation(
                    out=params[0:96, 0:tw], in_=psC[0:96, 0:tw], func=AF.Identity))
                s_par = S.fence("scalar", c_s)
                dt_s = {}
                for dhi in range(16):
                    S.wait("tensor", c_s, s_par)
                    if dhi >= 2:
                        S.wait("tensor", c_s, dt_s[dhi - 2])
                    bank = psD if dhi % 2 == 0 else psE
                    S.op("tensor", lambda e, dhi=dhi, tw=tw, bank=bank: e.matmul(
                        bank[:, 0:tw], s_dtpT[:, 128 * dhi:128 * (dhi + 1)], params[0:64, 0:tw],
                        start=True, stop=True))
                    t_dt = S.fence("tensor", c_t)
                    S.wait("scalar", c_t, t_dt)
                    S.op("scalar", lambda e, dhi=dhi, tw=tw, bank=bank: e.activation(
                        out=scr2[:, 0:tw], in_=bank[:, 0:tw], func=AF.Exp,
                        bias=s_dtpb[:, dhi:dhi + 1]))
                    S.op("scalar", lambda e, dhi=dhi, tw=tw: e.activation(
                        out=dtb[dhi][:, 0:tw], in_=scr2[:, 0:tw], func=AF.Ln, bias=1.0))
                    dt_s[dhi] = S.fence("scalar", c_s)
                s_dtall = dt_s[15]
                if debug and not is_halo:
                    S.wait("gpsimd", c_s, s_dtall)
                    for c in range(16):
                        S.dma("gpsimd", c_dbg, dbg["dt"][128 * c:128 * (c + 1), t0:t0 + tw],
                              dtb[c][:, 0:tw])
                    dbg_mark = S.fence("gpsimd", c_g)
                bc_v = {}
                for n in range(NST):
                    if n >= 1:
                        S.wait("scalar", c_t, bc_t)
                    S.dma("scalar", c_row, brow_t[0][:, 0:tw], params[64 + n:65 + n, 0:tw])
                    S.dma("scalar", c_row, crow_t[0][:, 0:tw], params[80 + n:81 + n, 0:tw])
                    S.wait("tensor", c_row)
                    if n >= 1:
                        S.wait("tensor", c_v, bc_v[n - 1])
                    S.op("tensor", lambda e, n=n, tw=tw: e.matmul(
                        psR[:, 0:tw], s_ones[:], brow_t[0][:, 0:tw], start=True, stop=True))
                    S.op("tensor", lambda e, n=n, tw=tw: e.matmul(
                        psC[:, 0:tw], s_ones[:], crow_t[0][:, 0:tw], start=True, stop=True))
                    bc_t = S.fence("tensor", c_t)
                    S.wait("vector", c_t, bc_t)
                    S.op("vector", lambda e, n=n, tw=tw: e.tensor_copy(
                        out=bbc[n][:, 0:tw], in_=psR[:, 0:tw]))
                    S.op("vector", lambda e, n=n, tw=tw: e.tensor_copy(
                        out=ccb[n][:, 0:tw], in_=psC[:, 0:tw]))
                    bc_v[n] = S.fence("vector", c_v)
                grp_v = {}
                grp_g = {}
                for dhi in range(16):
                    S.wait("vector", c_s, s_dtall)
                    S.op("vector", lambda e, dhi=dhi, tw=tw: e.tensor_tensor(
                        out=u_t[dhi % 2][:, 0:tw], in0=dtb[dhi][:, 0:tw],
                        in1=xs_c[dhi][:, 0:tw], op=OP.mult))
                    if is_halo:
                        S.op("vector", lambda e, dhi=dhi, tw=tw: e.tensor_scalar(
                            out=u_t[dhi % 2][:, 0:tw], in0=u_t[dhi % 2][:, 0:tw],
                            scalar1=s_umask[:], scalar2=None, op0=OP.mult))
                    v_u = S.fence("vector", c_v)
                    def yadd(gy, tw=tw):
                        dh2, n2 = gy // NST, gy % NST
                        S.wait("vector", c_g, grp_g[gy])
                        if n2 == 0:
                            S.op("vector", lambda e, dh2=dh2, n2=n2, tw=tw: e.tensor_copy(
                                out=ym_c[dh2][:, 0:tw], in_=sm_t[n2 % 4][:, 0:tw]))
                        else:
                            S.op("vector", lambda e, dh2=dh2, n2=n2, tw=tw: e.tensor_tensor(
                                out=ym_c[dh2][:, 0:tw], in0=sm_t[n2 % 4][:, 0:tw],
                                in1=ym_c[dh2][:, 0:tw], op=OP.add))
                    for n in range(NST):
                        g = dhi * NST + n
                        if g >= 4:
                            S.wait("scalar", c_v, grp_v[g - 4])
                        elif n == 0 and dhi == 0:
                            S.wait("scalar", c_v, bc_v[NST - 1])
                        S.op("scalar", lambda e, dhi=dhi, n=n, tw=tw: e.activation(
                            out=dec_t[n % 4][:, 0:tw], in_=dtb[dhi][:, 0:tw], func=AF.Exp,
                            scale=s_negA[:, 16 * dhi + n:16 * dhi + n + 1]))
                        s_dec = S.fence("scalar", c_s)
                        S.wait("vector", c_s, s_dec)
                        S.op("vector", lambda e, dhi=dhi, n=n, tw=tw: e.tensor_tensor(
                            out=uu_t[n % 4][:, 0:tw], in0=u_t[dhi % 2][:, 0:tw],
                            in1=bbc[n][:, 0:tw], op=OP.mult))
                        if g >= 4:
                            S.wait("vector", c_g, grp_g[g - 4])
                        elif oi > 0 and g == 0:
                            S.wait("vector", c_g, last_grp_g)
                        S.op("vector", lambda e, g=g, n=n, tw=tw: e.tensor_tensor_scan(
                            out=ss_t[n % 4][:, 0:tw], data0=dec_t[n % 4][:, 0:tw],
                            data1=uu_t[n % 4][:, 0:tw], initial=carry[:, g:g + 1],
                            op0=OP.mult, op1=OP.add))
                        grp_v[g] = S.fence("vector", c_v)
                        if g >= 2:
                            yadd(g - 2)
                        S.wait("gpsimd", c_v, grp_v[g])
                        S.op("gpsimd", lambda e, n=n, tw=tw: e.tensor_tensor(
                            out=sm_t[n % 4][:, 0:tw], in0=ss_t[n % 4][:, 0:tw],
                            in1=ccb[n][:, 0:tw], op=OP.mult))
                        S.op("gpsimd", lambda e, g=g, n=n, tw=tw: e.tensor_copy(
                            out=carry[:, g:g + 1], in_=ss_t[n % 4][:, tw - 1:tw]))
                        grp_g[g] = S.fence("gpsimd", c_g)
                    yadd(dhi * NST + 14)
                    yadd(dhi * NST + 15)
                    S.op("vector", lambda e, dhi=dhi, tw=tw: e.scalar_tensor_tensor(
                        out=ym_c[dhi][:, 0:tw], in0=xs_c[dhi][:, 0:tw],
                        scalar=s_dvec[:, dhi:dhi + 1], in1=ym_c[dhi][:, 0:tw],
                        op0=OP.mult, op1=OP.add))
                    if not is_halo:
                        S.wait("vector", c_s, s_sgy)
                        S.op("vector", lambda e, dhi=dhi, tw=tw: e.tensor_tensor(
                            out=ym_c[dhi][:, 0:tw], in0=ym_c[dhi][:, 0:tw],
                            in1=gt_c[dhi][:, 0:tw], op=OP.mult))
                    S.op("vector", lambda e: e.drain())
                last_grp_g = grp_g[255]
                v_ym = c_v.v
                g_ym = c_g.v
                if debug and not is_halo:
                    S.wait("gpsimd", c_v, v_ym)
                    for c in range(16):
                        S.dma("gpsimd", c_dbg, dbg["ym"][128 * c:128 * (c + 1), t0:t0 + tw],
                              ym_c[c][:, 0:tw])
                    dbg_mark = S.fence("gpsimd", c_g)
                if is_halo:
                    continue
                opev = {}
                for hc in range(8):
                    bi = hc % 2
                    bi2 = 2 + hc % 2
                    S.wait("sync", c_t, wbf_mark.get(bi, 0))
                    S.dma("sync", c_ld, mbfg[bi][:].rearrange("p (k f) -> p k f", k=8),
                          out_block(hc, 0))
                    S.wait("sync", c_t, wbf_mark.get(bi2, 0))
                    S.dma("sync", c_ld, mbfg[bi2][:].rearrange("p (k f) -> p k f", k=8),
                          out_block(hc, 8))
                    stg_mark[bi] = c_ld.v
                    stg_mark[bi2] = c_ld.v
                    S.wait("tensor", c_ld, stg_mark[bi2])
                    if hc == 0:
                        S.wait("tensor", c_v, v_ym)
                        S.wait("tensor", c_g, g_ym)
                    if hc >= 2:
                        S.wait("tensor", c_v, opev[hc - 2])
                    bank = psA if hc % 2 == 0 else psB
                    for k in range(16):
                        wb = mbfg[bi] if k < 8 else mbfg[bi2]
                        S.op("tensor", lambda e, wb=wb, k=k, bank=bank, tw=tw: e.matmul(
                            bank[:, 0:tw], wb[:, 128 * (k % 8):128 * (k % 8 + 1)],
                            ym_c[k][:, 0:tw], start=(k == 0), stop=(k == 15)))
                    t_op = S.fence("tensor", c_t)
                    wbf_mark[bi] = t_op
                    wbf_mark[bi2] = t_op
                    S.wait("scalar", c_t, t_op)
                    S.op("scalar", lambda e, hc=hc, tw=tw, bank=bank: e.activation(
                        out=spre[hc][:, 0:tw], in_=bank[:, 0:tw], func=AF.Identity,
                        bias=s_outb[:, hc:hc + 1]))
                    s_op = S.fence("scalar", c_s)
                    S.wait("vector", c_s, s_op)
                    opev[hc] = S.fence("vector", c_v)
                s_spre = c_s.v
                for q in range(tw // 128):
                    pan = (t0 - HALO) // 128 + q
                    par = 0
                    S.wait("tensor", c_s, s_spre)
                    if 0 in fin_v:
                        S.wait("tensor", c_v, fin_v[0])
                    for c in range(8):
                        S.op("tensor", lambda e, c=c, q=q: e.transpose(
                            psT[:, 128 * c:128 * (c + 1)], spre[c][:, 128 * q:128 * (q + 1)],
                            s_eyeb[:]))
                    t_tr = S.fence("tensor", c_t)
                    if pan == 0:
                        S.wait("sync", c_st, g_spilled)
                    S.wait("sync", c_s, fin_s.get(par, 0))
                    S.wait("sync", c_v, fin_v.get(par, 0))
                    S.dma("sync", c_gs, gtp2[par][:], d_gsp[128 * pan:128 * (pan + 1), :])
                    S.dma("sync", c_gs, xfin[par][:], d_xin[128 * (pan + 1):128 * (pan + 2), :])
                    mk_g = c_gs.v
                    S.wait("vector", c_t, t_tr)
                    S.wait("vector", c_st, fin_st.get(par, 0))
                    S.wait("vector", c_gs, mk_g)
                    S.op("vector", lambda e, par=par, pan=pan: e.tensor_scalar(
                        out=xnf[:], in0=xfin[par][:], scalar1=mr3[:, pan + 1:pan + 2, 0],
                        scalar2=mr3[:, pan + 1:pan + 2, 1], op0=OP.subtract, op1=OP.mult))
                    S.op("vector", lambda e: e.drain())
                    S.op("vector", lambda e: e.tensor_tensor(
                        out=sfin[:], in0=psT[:], in1=xnf[:], op=OP.add))
                    v_tr = S.fence("vector", c_v)
                    S.wait("scalar", c_gs, mk_g)
                    S.wait("scalar", c_v, v_tr)
                    S.op("scalar", lambda e, par=par: e.activation(
                        out=scr2[:], in_=gtp2[par][:], func=AF.Silu))
                    s_fin = S.fence("scalar", c_s)
                    fin_s[par] = s_fin
                    S.wait("vector", c_s, s_fin)
                    S.op("vector", lambda e, par=par: e.tensor_tensor(
                        out=ofin[par][:], in0=sfin[:], in1=scr2[:], op=OP.mult))
                    S.op("vector", lambda e: e.drain())
                    S.op("vector", lambda e, par=par: e.tensor_tensor(
                        out=ofin[par][:], in0=ofin[par][:], in1=xfin[par][:], op=OP.add))
                    fin_v[par] = S.fence("vector", c_v)
                    first_fin = False
                    S.wait("gpsimd", c_v, fin_v[par])
                    fin_st[par] = S.dma("gpsimd", c_st, d_out[128 * pan:128 * (pan + 1), :],
                                        ofin[par][:])
                    if debug:
                        S.dma("gpsimd", c_dbg, dbg["spre"][128 * pan:128 * (pan + 1), :], sfin[:])
                        dbg_mark = S.fence("gpsimd", c_g)
                        S.wait("vector", c_g, dbg_mark)

            S.wait("gpsimd", c_st, c_st.v)

        block = ctx.enter_context(nc.Block())

        @block.sync
        def _(sync):
            for eng, fn in S.items:
                if eng == "sync":
                    fn(sync)

        @block.vector
        def _(vector):
            for eng, fn in S.items:
                if eng == "vector":
                    fn(vector)

        @block.scalar
        def _(scalar):
            for eng, fn in S.items:
                if eng == "scalar":
                    fn(scalar)

        @block.tensor
        def _(tensor):
            for eng, fn in S.items:
                if eng == "tensor":
                    fn(tensor)

        @block.gpsimd
        def _(gpsimd):
            for eng, fn in S.items:
                if eng == "gpsimd":
                    fn(gpsimd)

    return nc


def _prep_inputs(inputs):
    x = np.ascontiguousarray(np.asarray(inputs["L0_attn_hidden"])[0])
    asc = np.ascontiguousarray

    def vec2(v, cols):
        return asc(np.asarray(v).reshape(cols, 128).T)

    import ml_dtypes
    common = dict(
        winT=asc(np.asarray(inputs["L0_attn_in_w"]).T),
        inb=vec2(inputs["L0_attn_in_b"], 32),
        convw=asc(np.asarray(inputs["L0_attn_conv_w"])[:, 0, :].reshape(16, 128, 4)),
        convb=vec2(inputs["L0_attn_conv_b"], 16),
        xpT=asc(np.asarray(inputs["L0_attn_xp_w"]).T),
        dtpT=asc(np.asarray(inputs["L0_attn_dtp_w"]).T),
        dtpb=vec2(inputs["L0_attn_dtp_b"], 16),
        alog=asc(np.asarray(inputs["L0_attn_A_log"])),
        dvec=vec2(inputs["L0_attn_D"], 16),
        outT=asc(np.asarray(inputs["L0_attn_out_w"]).T),
        outb=vec2(inputs["L0_attn_out_b"], 8),
        lnw=vec2(inputs["L0_attn_ln_w"], 8),
        lnb=vec2(inputs["L0_attn_ln_b"], 8),
        gam=vec2(inputs["L0_ffn_ln_gamma"], 8),
        bet=vec2(inputs["L0_ffn_ln_beta"], 8),
        rT=asc(np.asarray(inputs["L0_ffn_router_w"]).T),
        guT=asc(np.asarray(inputs["L0_ffn_gate_up_w"]).transpose(0, 2, 1)),
        dnT=asc(np.asarray(inputs["L0_ffn_down_w"]).transpose(0, 2, 1)),
        sguT=asc(np.asarray(inputs["L0_ffn_shared_gate_up_w"]).T),
        sdnT=asc(np.asarray(inputs["L0_ffn_shared_down_w"]).T),
        eyeb=np.eye(128, dtype=ml_dtypes.bfloat16),
        eyef=np.eye(128, dtype=np.float32),
        ones1=np.ones((1, 128), dtype=np.float32),
    )
    in_maps = []
    for k in range(NCORE):
        m = dict(common)
        if k == 0:
            halo = np.zeros((HALO, H), dtype=np.float32)
            m["umask"] = np.zeros((128, 1), dtype=np.float32)
        else:
            halo = x[T * k - HALO:T * k]
            m["umask"] = np.ones((128, 1), dtype=np.float32)
        m["xin"] = np.ascontiguousarray(np.concatenate([halo, x[T * k:T * (k + 1)]], axis=0))
        in_maps.append(m)
    return in_maps


def kernel(**inputs):
    in_maps = _prep_inputs(inputs)
    nc = build_nc(debug=DEBUG)
    res = run_bass_kernel_spmd(nc, in_maps, core_ids=list(range(NCORE)))
    out = np.concatenate([np.asarray(r["out"]) for r in res.results], axis=0)
    if DEBUG:
        kernel.last_results = res.results
    return out.reshape(1, NCORE * T, H).astype(np.float32)
